# version 04
# baseline (speedup 1.0000x reference)
import sys
try:
    import concourse.bass
except ImportError:
    sys.path.insert(0, "/opt/trn_rl_repo")
import numpy as np
import ml_dtypes
import concourse.bass as bass
import concourse.mybir as mybir
from concourse.bass_utils import run_bass_kernel_spmd

F32 = mybir.dt.float32
BF16 = mybir.dt.bfloat16
FP8 = mybir.dt.float8e4
AF = mybir.ActivationFunctionType
ALU = mybir.AluOpType
AX = mybir.AxisListType
GROUPS = [[0, 1, 2, 3], [4, 5, 6, 7]]

D, NST, RDT, NE, FF = 2048, 16, 128, 4, 8192
T, L = 1024, 4096
NCk = 16
TCH, NTC = 128, 32
FG, NFG = 256, 32
NBLK = 4 * NTC
AGB = 2176


class Chan:
    def __init__(self, es, nc, name):
        self.sem = es.enter_context(nc.semaphore(name))
        self.n = 0

    def inc(self, k=1):
        self.n += k
        return self.n


class VChan:
    def __init__(self, oc):
        self.oc = oc
        self.n = 0
        self.marks = [0]

    def mark(self):
        self.n += 1
        self.marks.append(self.oc.n)
        return self.n


def build_program(debug=False, do_scan=True, do_out=True, nfg=NFG):
    nc = bass.Bass("TRN2", target_bir_lowering=False, debug=False)

    x_in = nc.dram_tensor("x", [T, D], F32, kind="ExternalInput").ap()
    xpwT = nc.dram_tensor("xpwT", [D, 160], F32, kind="ExternalInput").ap()
    dtwT = nc.dram_tensor("dtwT", [RDT, D], F32, kind="ExternalInput").ap()
    dtb = nc.dram_tensor("dtb", [128, 16], F32, kind="ExternalInput").ap()
    aslc = nc.dram_tensor("aslc", [128, 64], F32, kind="ExternalInput").ap()
    dvec = nc.dram_tensor("dvec", [128, 16], F32, kind="ExternalInput").ap()
    opwTh = nc.dram_tensor("opwT", [D, D], F32, kind="ExternalInput").ap()
    opb = nc.dram_tensor("opb", [128, 16], F32, kind="ExternalInput").ap()
    wr_h = nc.dram_tensor("wr", [128, 64], F32, kind="ExternalInput").ap()
    br_h = nc.dram_tensor("br", [NE, 1], F32, kind="ExternalInput").ap()
    wg_h = nc.dram_tensor("wg", [NE, FF // 128, 128, D], F32, kind="ExternalInput").ap()
    wu_h = nc.dram_tensor("wu", [NE, FF // 128, 128, D], F32, kind="ExternalInput").ap()
    wd_h = nc.dram_tensor("wd", [NE, FF, D], F32, kind="ExternalInput").ap()
    id32 = nc.dram_tensor("id32", [128, 128], F32, kind="ExternalInput").ap()
    oneh = nc.dram_tensor("oneh", [1, 128], BF16, kind="ExternalInput").ap()

    out_h = nc.dram_tensor("out", [T, D], F32, kind="ExternalOutput").ap()

    ag1f_i = nc.dram_tensor("ag1f_i", [D, T], F32, kind="Internal", addr_space="Local")
    ag1f_o = nc.dram_tensor("ag1f_o", [4 * D, T], F32, kind="Internal", addr_space="Local")
    ag1b_i = nc.dram_tensor("ag1b_i", [D, T], BF16, kind="Internal", addr_space="Local")
    ag1b_o = nc.dram_tensor("ag1b_o", [4 * D, T], BF16, kind="Internal", addr_space="Local")
    agc_i = nc.dram_tensor("agc_i", [32, T], BF16, kind="Internal", addr_space="Local")
    agc_o = nc.dram_tensor("agc_o", [128, T], BF16, kind="Internal", addr_space="Local")
    ag2_i = nc.dram_tensor("ag2_i", [512, L], BF16, kind="Internal", addr_space="Local")
    tw_d = nc.dram_tensor("tw_d", [1, T], F32, kind="Internal", addr_space="Local")
    mk_d = nc.dram_tensor("mk_d", [4, T], F32, kind="Internal", addr_space="Local")
    sc_f = nc.dram_tensor("sc_f", [512, L], F32, kind="Internal", addr_space="Local")
    sc_b = nc.dram_tensor("sc_b", [512, L], BF16, kind="Internal", addr_space="Local")
    y_scr = nc.dram_tensor("y_scr", [D, T], BF16, kind="Internal", addr_space="Local")
    wgb_i = nc.dram_tensor("wgb_i", [28, 262144], BF16, kind="Internal", addr_space="Local")
    wub_i = nc.dram_tensor("wub_i", [28, 262144], BF16, kind="Internal", addr_space="Local")
    wdb_i = nc.dram_tensor("wdb_i", [28, 262144], BF16, kind="Internal", addr_space="Local")
    wgb_o = nc.dram_tensor("wgb_o", [224, 262144], BF16, kind="Internal", addr_space="Shared")
    wub_o = nc.dram_tensor("wub_o", [224, 262144], BF16, kind="Internal", addr_space="Shared")
    wdb_o = nc.dram_tensor("wdb_o", [224, 262144], BF16, kind="Internal", addr_space="Shared")
    opwb = nc.dram_tensor("opwb", [D, D], BF16, kind="Internal", addr_space="Local")
    ag2_o = nc.dram_tensor("ag2_o", [D, L], BF16, kind="Internal", addr_space="Local")

    dbg = {}
    if debug:
        def dbgt(name, shape, dt):
            dbg[name] = nc.dram_tensor("dbg_" + name, shape, dt, kind="ExternalOutput").ap()
        dbgt("xnT", [128, NCk * T], BF16)
        dbgt("delta", [D, T], F32)
        dbgt("bc", [32, T], BF16)
        dbgt("y", [512, L], BF16)
        dbgt("logits", [4, T], F32)
        dbgt("mask", [4, T], F32)
        dbgt("topw", [1, T], F32)
        dbgt("acc", [128, NCk * T], F32)

    from contextlib import ExitStack
    with ExitStack() as ES:
        e_ = ES.enter_context

        xnT = e_(nc.sbuf_tensor("xnT", [128, NCk * T], BF16))
        acc = e_(nc.sbuf_tensor("acc", [128, NCk * T], F32))
        arena = e_(nc.sbuf_tensor("arena", [128, 30720], BF16))
        wd_sb = arena[:, 0:16384]
        h_sb = arena[:, 16384:24576]
        wring = [arena[:, 24576 + i * 2048:24576 + (i + 1) * 2048] for i in range(3)]
        opw_sb = arena[:, 0:16384]
        ysb = arena[:, 16384:24576]

        scn = e_(nc.sbuf_tensor("scn", [128, 16384], BF16))
        dA_s = scn[:, 0:2048]
        u_s = scn[:, 2048:4096]
        trj = scn[:, 4096:6144]
        m_s = scn[:, 6144:8192]
        bbc = scn[:, 8192:10240]
        cbc = scn[:, 10240:12288]
        xsc = [scn[:, 12288 + i * TCH:12288 + (i + 1) * TCH] for i in range(2)]
        zsc = scn[:, 12544:12672]
        ysc = [scn[:, 12672 + i * TCH:12672 + (i + 1) * TCH] for i in range(2)]
        dltf = scn[:, 13312:14336].bitcast(F32)
        xtm = scn[:, 0:8192].bitcast(F32)
        sqscr = scn[:, 8192:12288].bitcast(F32)
        dstage = scn[:, 12288:16384].bitcast(F32)
        xrel = scn[:, 0:8192].bitcast(F32)
        outst = scn[:, 8192:16384].bitcast(F32)
        mfm1 = scn[0:1, 12288:16384]
        cmb_f = sqscr[0:5, 1024:2048]

        spf = e_(nc.sbuf_tensor("spf", [128, 1416], F32))
        stats = spf[:, 0:8]
        ppv = spf[:, 8:56]
        dtb_pp, dv_pp, ob_pp = ppv[:, 0:16], ppv[:, 16:32], ppv[:, 32:48]
        a_sb = spf[:, 56:120]
        lgtm = spf[:, 120:152]
        mtm = spf[:, 152:192]
        hstate = spf[:, 192:256]
        id_sb = spf[:, 256:384]
        lgfm = spf[0:4, 384:1408]

        spb = e_(nc.sbuf_tensor("spb", [128, 6848], BF16))
        msk = spb[:, 0:2048].bitcast(FP8)
        twbc = spb[:, 0:2048].bitcast(F32)
        silu_t = spb[:, 2048:3072]
        um_t = spb[:, 3072:4096]
        relu2 = spb[:, 2048:3072]
        dtlT = spb[:, 4608:5632]
        bc_sb = spb[0:32, 5632:6656]
        wr_sb = spb[:, 6656:6720]
        oh_sb = spb[0:1, 6720:6848]

        br_sb = spf[0:4, 7:8]
        eps_ap = spf[:, 1408:1409]

        pg = [e_(nc.psum_tensor(f"pg{i}", [128, 512], F32)) for i in range(2)]
        pu = [e_(nc.psum_tensor(f"pu{i}", [128, 512], F32)) for i in range(2)]
        py = [e_(nc.psum_tensor(f"py{i}", [128, 512], F32)) for i in range(2)]
        sp = [e_(nc.psum_tensor(f"sp{i}", [128, 512], F32)) for i in range(2)]

        ch = {}
        for name in ("x", "wsm", "tp", "te", "ag1in",
                     "ag", "wd", "te_gu", "te_y",
                     "scb", "ag2",
                     "ysbL", "ow", "te_f", "scp", "mk", "twb", "mf1", "wcst", "agw", "dbgs",
                     "wgs0", "wgs1", "wgs2", "wds",
                     "xt0", "xt1", "dst0", "dst1", "scf0", "scf1",
                     "wg0", "wg1", "wg2", "yst0", "yst1", "out0", "out1",
                     "xr0", "xr1"):
            ch[name] = Chan(ES, nc, "c_" + name)
        ch_xt = [ch["xt0"], ch["xt1"]]
        ch_dst = [ch["dst0"], ch["dst1"]]
        ch_scf = [ch["scf0"], ch["scf1"]]
        ch_wg = [ch["wg0"], ch["wg1"], ch["wg2"]]
        ch_yst = [ch["yst0"], ch["yst1"]]
        ch_out = [ch["out0"], ch["out1"]]
        ch_xr = [ch["xr0"], ch["xr1"]]
        ch_wgs = [ch["wgs0"], ch["wgs1"], ch["wgs2"]]

        Q = {k: [] for k in ("sync", "gps", "te", "v", "s")}
        regs = {}
        last = {k: [None] for k in Q}
        order = {"v": Chan(ES, nc, "ord_v"), "s": Chan(ES, nc, "ord_s")}
        for name in ("ln", "xnT", "v", "v_h", "v_y", "v_sc", "v_ys", "v_f", "v_u"):
            ch[name] = VChan(order["v"])
        for name in ("s", "s_t", "s_sc"):
            ch[name] = VChan(order["s"])

        def EM(q, fn):
            if q in order:
                oc = order[q]
                n = oc.n
                if n > 0:
                    Q[q].append(lambda e, sem=oc.sem, val=n: e.wait_ge(sem, val))

                def run(e, fn=fn, q=q, oc=oc):
                    ins = fn(e)
                    last[q][0] = ins
                    ins.then_inc(oc.sem, 1)
                    return ins
                Q[q].append(run)
                oc.inc()
            else:
                def run(e, fn=fn, q=q):
                    last[q][0] = fn(e)
                Q[q].append(run)

        def SY(fn): EM("sync", fn)
        def GP(fn): EM("gps", fn)
        def TE(fn): EM("te", fn)
        def V(fn): EM("v", fn)
        def SC(fn): EM("s", fn)

        def AT(q, c, k=1):
            if isinstance(c, VChan):
                assert k == 1
                c.mark()
            else:
                Q[q].append(lambda e, q=q, c=c, k=k: last[q][0].then_inc(c.sem, k))
                c.inc(k)

        def emit_blocks():
            with nc.Block() as block:
                @block.sync
                def _(e):
                    for fn in Q["sync"]:
                        fn(e)

                @block.gpsimd
                def _(e):
                    for fn in Q["gps"]:
                        fn(e)

                @block.tensor
                def _(e):
                    for fn in Q["te"]:
                        fn(e)

                @block.vector
                def _(e):
                    for fn in Q["v"]:
                        fn(e)

                @block.scalar
                def _(e):
                    for fn in Q["s"]:
                        fn(e)

        def W(q, c, v=None):
            val = c.n if v is None else v
            if val <= 0:
                return
            if isinstance(c, VChan):
                oval = c.marks[val]
                if oval <= 0:
                    return
                Q[q].append(lambda e, sem=c.oc.sem, val=oval: e.wait_ge(sem, val))
            else:
                Q[q].append(lambda e, sem=c.sem, val=val: e.wait_ge(sem, val))

        def ld(q, out, in_, c, **kw):
            def run(e, out=out, in_=in_, kw=kw, sem=c.sem, q=q):
                last[q][0] = e.dma_start(out=out, in_=in_, **kw).then_inc(sem, 16)
            Q[q].append(run)
            c.inc(16)



        def _rk(e):
            regs["rk"] = e.cc_rank(GROUPS)
        SY(_rk)

        ld("sync", dtb_pp, dtb[:, :], ch["x"])
        ld("sync", dv_pp, dvec[:, :], ch["x"])
        ld("sync", ob_pp, opb[:, :], ch["x"])
        ld("sync", a_sb, aslc[:, :], ch["x"])
        ld("sync", br_sb, br_h[:, :], ch["x"])
        ld("sync", id_sb, id32[:, :], ch["x"])
        smalls = ch["x"].n

        def _rk8(e):
            regs["rk8"] = e.cc_rank([[0, 1, 2, 3, 4, 5, 6, 7]])
        GP(_rk8)
        for ti, (srcT, dstT) in enumerate(((wg_h, wgb_i), (wu_h, wub_i), (wd_h, wdb_i))):
            for ei in range(4):
                if ti < 2:
                    srcv = srcT[ei, 8:64].rearrange("f p t -> (f p t)")
                else:
                    srcv = srcT[ei, 1024:8192, :].rearrange("p t -> (p t)")
                srcv = srcv.rearrange("(r x) -> r x", r=8)
                dstv = dstT.ap()[ei * 7:(ei + 1) * 7, :].rearrange("a b -> (a b)").unsqueeze(0)
                def _cst(e, srcv=srcv, dstv=dstv):
                    e.dma_start(out=dstv, in_=srcv[bass.ts(regs["rk8"], 1)],
                                max_dma_last_dim=8192).then_inc(ch["wcst"].sem, 16)
                GP(_cst)
                ch["wcst"].inc(16)
        def _cst_opw(e):
            e.dma_start(out=opwb.ap()[:, :], in_=opwTh[:, :],
                        max_dma_last_dim=8192).then_inc(ch["wcst"].sem, 16)
        GP(_cst_opw)
        ch["wcst"].inc(16)
        wcst_done = ch["wcst"].n
        ld("gps", (xpw_sb := spb[:, 0:2560]).rearrange("p (c m) -> p c m", c=16), xpwT[:, :].rearrange("(c p) m -> p c m", p=128), ch["wsm"])
        ld("gps", dtw_sb := spb[:, 2560:4608], dtwT[:, :], ch["wsm"], max_dma_last_dim=4096)
        ld("gps", wr_sb, wr_h[:, :], ch["wsm"])
        ld("gps", oh_sb, oneh[:, :], ch["wsm"])
        wsm_done = ch["wsm"].n

        V(lambda e: e.memset(eps_ap, 1e-5))
        W("s", ch["x"], smalls)
        SC(lambda e: e.activation(out=sqscr[:, 0:64], in_=a_sb, func=AF.Exp))
        AT("s", ch["s"])
        a_sc = ch["s"].n
        W("v", ch["s"], a_sc)
        V(lambda e: e.tensor_scalar_mul(out=a_sb, in0=sqscr[:, 0:64], scalar1=-1.0))
        AT("v", ch["v"])

        xnT3 = xnT[:].rearrange("p (c t) -> p c t", c=NCk)
        for tt in range(8):
            b = xtm[:, (tt % 2) * 2048:(tt % 2 + 1) * 2048]
            if tt >= 2:
                W("sync", ch["tp"], tt - 1)
            ld("sync", b, x_in[tt * 128:(tt + 1) * 128, :], ch_xt[tt % 2])
            W("v", ch_xt[tt % 2], ch_xt[tt % 2].n)
            V(lambda e, b=b: e.reduce_sum(out=stats[:, 0:1], in_=b, axis=AX.X))
            V(lambda e: e.tensor_scalar_mul(out=stats[:, 1:2], in0=stats[:, 0:1], scalar1=1.0 / D))
            V(lambda e, b=b: e.tensor_scalar(out=b, in0=b, scalar1=stats[:, 1:2], scalar2=None, op0=ALU.subtract))
            AT("v", ch["v"])
            vc = ch["v"].n
            W("s", ch["v"], vc)
            SC(lambda e, b=b: e.activation(out=sqscr[:, 0:1024], in_=b[:, 0:1024], func=AF.Square, accum_out=stats[:, 2:3]))
            SC(lambda e, b=b: e.activation(out=sqscr[:, 1024:2048], in_=b[:, 1024:2048], func=AF.Square, accum_out=stats[:, 3:4]))
            AT("s", ch["s"])
            sc1 = ch["s"].n
            W("v", ch["s"], sc1)
            V(lambda e: e.tensor_add(out=stats[:, 4:5], in0=stats[:, 2:3], in1=stats[:, 3:4]))
            V(lambda e: e.tensor_scalar_mul(out=stats[:, 4:5], in0=stats[:, 4:5], scalar1=1.0 / D))
            AT("v", ch["v"])
            vc2 = ch["v"].n
            W("s", ch["v"], vc2)
            SC(lambda e: e.activation(out=stats[:, 5:6], in_=stats[:, 4:5], func=AF.Sqrt, bias=eps_ap))
            AT("s", ch["s"])
            sc2 = ch["s"].n
            W("v", ch["s"], sc2)
            V(lambda e: e.reciprocal(out=stats[:, 6:7], in_=stats[:, 5:6]))
            V(lambda e, b=b: e.tensor_scalar(out=b, in0=b, scalar1=stats[:, 6:7], scalar2=None, op0=ALU.mult))
            AT("v", ch["ln"])
            W("te", ch["ln"], ch["ln"].n)
            if tt == 0:
                W("te", ch["x"], smalls)
            if tt > 0:
                W("te", ch["xnT"], tt)
            for dc in range(16):
                bank = (pg + pu)[dc // 4]
                TE(lambda e, b=b, dc=dc, bank=bank: e.transpose(
                    out=bank.ap()[:, (dc % 4) * 128:(dc % 4 + 1) * 128],
                    in_=b[:, dc * 128:(dc + 1) * 128], identity=id_sb))
            AT("te", ch["tp"])
            W("v", ch["tp"], ch["tp"].n)
            for g in range(4):
                bank = (pg + pu)[g]
                V(lambda e, g=g, bank=bank, tt=tt: e.tensor_copy(
                    out=xnT3[:, g * 4:(g + 1) * 4, tt * 128:(tt + 1) * 128],
                    in_=bank.ap()[:]))
            AT("v", ch["xnT"])
        xnT_done = ch["xnT"].n

        TE(lambda e: e.drain())

        W("sync", ch["xnT"], xnT_done)
        ld("sync", ag1b_i.ap()[0:D, :].rearrange("(c p) t -> p c t", p=128), xnT[:].rearrange("p (c t) -> p c t", c=16), ch["ag1in"])

        W("te", ch["wsm"], wsm_done)
        W("te", ch["xnT"], xnT_done)
        for th in range(2):
            for c in range(NCk):
                TE(lambda e, th=th, c=c: e.matmul(
                    out=py[th].ap()[:], lhsT=xpw_sb[:, c * 160:c * 160 + 128],
                    rhs=xnT[:, c * T + th * 512:c * T + (th + 1) * 512],
                    start=(c == 0), stop=(c == NCk - 1)))
        AT("te", ch["te"])
        te_dtl = ch["te"].n
        W("v", ch["te"], te_dtl)
        for th in range(2):
            V(lambda e, th=th: e.tensor_copy(out=dtlT[:, th * 512:(th + 1) * 512], in_=py[th].ap()[:]))
        AT("v", ch["v"])
        v_dtl = ch["v"].n

        for th in range(2):
            for c in range(NCk):
                TE(lambda e, th=th, c=c: e.matmul(
                    out=pg[th].ap()[0:32, :], lhsT=xpw_sb[:, c * 160 + 128:c * 160 + 160],
                    rhs=xnT[:, c * T + th * 512:c * T + (th + 1) * 512],
                    start=(c == 0), stop=(c == NCk - 1)))
        AT("te", ch["te"])
        te_bc = ch["te"].n
        W("v", ch["te"], te_bc)
        for th in range(2):
            V(lambda e, th=th: e.tensor_copy(out=bc_sb[:, th * 512:(th + 1) * 512], in_=pg[th].ap()[0:32, :]))
        AT("v", ch["v"])
        v_bc = ch["v"].n
        W("sync", ch["v"], v_bc)
        ld("sync", agc_i.ap()[:, :], bc_sb, ch["ag1in"])

        W("te", ch["v"], v_dtl)
        W("s", ch["x"], smalls)
        v_base = ch["v"].n
        for c2 in range(NCk):
            for th in range(2):
                u = 2 * c2 + th
                pb = pu[u % 2]
                if u >= 2:
                    W("te", ch["v"], v_base + (u - 1))
                TE(lambda e, c2=c2, th=th, pb=pb: e.matmul(
                    out=pb.ap()[:], lhsT=dtw_sb[:, c2 * 128:(c2 + 1) * 128],
                    rhs=dtlT[:, th * 512:(th + 1) * 512], start=True, stop=True))
                AT("te", ch["te"])
                tec = ch["te"].n
                W("s", ch["te"], tec)
                if u >= 2:
                    W("s", ch["v"], v_base + (u - 1))
                sp1 = sqscr[:, (u % 2) * 512:(u % 2) * 512 + 512]
                SC(lambda e, pb=pb, c2=c2, sp1=sp1: e.activation(
                    out=sp1, in_=pb.ap()[:], func=AF.Abs, bias=dtb_pp[:, c2:c2 + 1]))
                SC(lambda e, sp1=sp1: e.activation(out=sp1, in_=sp1, func=AF.Exp, scale=-1.0))
                SC(lambda e, sp1=sp1: e.activation(out=sp1, in_=sp1, func=AF.Ln, bias=1.0))
                AT("s", ch["s"])
                scc = ch["s"].n
                dsl = dstage[:, (c2 % 2) * 1024 + th * 512:(c2 % 2) * 1024 + (th + 1) * 512]
                W("v", ch["s"], scc)
                if c2 >= 2 and th == 0:
                    W("v", ch_dst[c2 % 2], 16 * (c2 // 2))
                V(lambda e, pb=pb, c2=c2, dsl=dsl: e.tensor_scalar(
                    out=dsl, in0=pb.ap()[:], scalar1=dtb_pp[:, c2:c2 + 1],
                    scalar2=0.0, op0=ALU.add, op1=ALU.max))
                V(lambda e, dsl=dsl, sp1=sp1: e.tensor_add(out=dsl, in0=dsl, in1=sp1))
                AT("v", ch["v"])
            W("sync", ch["v"], ch["v"].n)
            ld("sync", ag1f_i.ap()[c2 * 128:(c2 + 1) * 128, :],
               dstage[:, (c2 % 2) * 1024:(c2 % 2) * 1024 + 1024], ch_dst[c2 % 2])
        ag1in_done = ch["ag1in"].n

        W("te", ch["v"], v_dtl)
        for th in range(2):
            for c in range(NCk):
                TE(lambda e, th=th, c=c: e.matmul(
                    out=py[th].ap()[0:4, :], lhsT=wr_sb[:, c * 4:(c + 1) * 4],
                    rhs=xnT[:, c * T + th * 512:c * T + (th + 1) * 512],
                    start=(c == 0), stop=(c == NCk - 1)))
        AT("te", ch["te"])
        te_lg = ch["te"].n
        W("v", ch["te"], te_lg)
        for th in range(2):
            V(lambda e, th=th: e.tensor_scalar(
                out=lgfm[:, th * 512:(th + 1) * 512], in0=py[th].ap()[0:4, :],
                scalar1=br_sb, scalar2=None, op0=ALU.add))
        AT("v", ch["v"])
        v_lg = ch["v"].n

        TE(lambda e: e.drain())
        W("te", ch["v"], v_lg)
        for tt in range(8):
            TE(lambda e, tt=tt: e.transpose(
                out=pg[0].ap()[:, tt * 4:(tt + 1) * 4],
                in_=lgfm[:, tt * 128:(tt + 1) * 128], identity=id_sb[0:4, 0:4]))
        AT("te", ch["te"])
        te_lgt = ch["te"].n
        W("v", ch["te"], te_lgt)
        V(lambda e: e.tensor_copy(out=lgtm, in_=pg[0].ap()[:, 0:32]))
        for tt in range(8):
            lg4 = lgtm[:, tt * 4:(tt + 1) * 4]
            V(lambda e, lg4=lg4: e.reduce_max(out=stats[:, 0:1], in_=lg4, axis=AX.X))
            V(lambda e: e.tensor_scalar_mul(out=stats[:, 1:2], in0=stats[:, 0:1], scalar1=-1.0))
            AT("v", ch["v"])
            vmx = ch["v"].n
            W("s", ch["v"], vmx)
            SC(lambda e, lg4=lg4: e.activation(
                out=sqscr[:, 0:4], in_=lg4, func=AF.Exp,
                bias=stats[:, 1:2], accum_out=stats[:, 2:3]))
            AT("s", ch["s"])
            sse = ch["s"].n
            W("v", ch["s"], sse)
            V(lambda e, tt=tt: e.reciprocal(out=mtm[:, tt * 5 + 4:tt * 5 + 5], in_=stats[:, 2:3]))
            V(lambda e: e.memset(stats[:, 3:4], 1.0))
            for ei in range(4):
                V(lambda e, lg4=lg4, ei=ei: e.tensor_scalar(
                    out=stats[:, 4:5], in0=lg4[:, ei:ei + 1], scalar1=stats[:, 0:1],
                    scalar2=None, op0=ALU.is_equal))
                V(lambda e, tt=tt, ei=ei: e.tensor_mul(
                    out=mtm[:, tt * 5 + ei:tt * 5 + ei + 1], in0=stats[:, 4:5], in1=stats[:, 3:4]))
                V(lambda e, tt=tt, ei=ei: e.tensor_sub(
                    out=stats[:, 3:4], in0=stats[:, 3:4], in1=mtm[:, tt * 5 + ei:tt * 5 + ei + 1]))
        AT("v", ch["v"])
        v_msk = ch["v"].n
        W("te", ch["v"], v_msk)
        for tt in range(8):
            TE(lambda e, tt=tt: e.transpose(
                out=pu[tt // 4].ap()[0:5, (tt % 4) * 128:(tt % 4 + 1) * 128],
                in_=mtm[:, tt * 5:tt * 5 + 5], identity=id_sb))
        AT("te", ch["te"])
        te_mkt = ch["te"].n
        TE(lambda e: e.drain())
        W("v", ch["te"], te_mkt)
        for g in range(2):
            V(lambda e, g=g: e.tensor_copy(out=cmb_f[:, g * 512:(g + 1) * 512], in_=pu[g].ap()[0:5, :]))
        AT("v", ch["v"])
        v_mfm = ch["v"].n
        W("sync", ch["v"], v_mfm)
        ld("sync", tw_d.ap()[:, :], cmb_f[4:5, :], ch["mk"])
        ld("sync", mk_d.ap()[:, :], cmb_f[0:4, :], ch["mk"])
        mk_d_ld = ch["mk"].n
        mfm1_ch = ch["mf1"]
        W("te", ch["v"], v_mfm)
        W("te", mfm1_ch, 16)
        for ei in range(4):
            if ei > 0:
                W("te", ch["v"], ch["v"].n)
            for th in range(2):
                TE(lambda e, ei=ei, th=th: e.matmul(
                    out=py[th].ap()[:], lhsT=oh_sb,
                    rhs=mfm1[0:1, ei * 1024 + th * 512:ei * 1024 + (th + 1) * 512],
                    start=True, stop=True))
            AT("te", ch["te"])
            temb = ch["te"].n
            W("v", ch["te"], temb)
            for th in range(2):
                V(lambda e, ei=ei, th=th: e.tensor_copy(
                    out=msk[:, ei * T + th * 512:ei * T + (th + 1) * 512],
                    in_=py[th].ap()[:]))
            AT("v", ch["v"])
        v_mskb = ch["v"].n
        te_guard_mask = [ch["te"].n]

        if debug:
            W("sync", ch["v"], v_mskb)
            W("sync", ch["mk"], ch["mk"].n)
            ld("sync", dbg["logits"][:, :], lgfm, ch["dbgs"])
            ld("sync", dbg["mask"][:, :], cmb_f[0:4, :], ch["dbgs"])
            ld("sync", dbg["topw"][:, :], tw_d.ap()[:, :], ch["dbgs"])
            ld("sync", dbg["xnT"][:, :], xnT[:], ch["dbgs"])
            W("sync", ch_dst[0], 16 * 8)
            W("sync", ch_dst[1], 16 * 8)
            ld("sync", dbg["delta"][:, :], ag1f_i.ap()[:, :], ch["dbgs"])
            W("sync", ch["ag1in"], ch["ag1in"].n)
            ld("sync", dbg["bc"][:, :], agc_i.ap()[:, :], ch["dbgs"])

        W("v", ch["x"], smalls)
        V(lambda e: e.memzero(hstate))
        V(lambda e: e.memzero(acc[:]))

        pump = {"g": None}

        def scan_pump(n=1):
            if not do_scan:
                return
            for _ in range(n):
                if pump["g"] is None:
                    if st["i"] >= NBLK:
                        return
                    pump["g"] = scan_block_gen()
                try:
                    next(pump["g"])
                except StopIteration:
                    pump["g"] = None

        st = {"i": 0}
        bbc3 = bbc.rearrange("p (n t) -> p n t", n=16)
        cbc3 = cbc.rearrange("p (n t) -> p n t", n=16)
        m3w = m_s.rearrange("p (t n) -> p n t", n=16)
        m3r = m_s.rearrange("p (t n) -> p t n", n=16)
        trj3 = trj.rearrange("p (n t) -> p n t", n=16)

        def stage_scan():
            W("sync", ch["ag"], 3)
            scf3 = sc_f.ap().rearrange("(k p) t -> k p t", k=4)
            scb3 = sc_b.ap().rearrange("(k p) t -> k p t", k=4)
            for j in range(4):
                def cpf(e, j=j):
                    v5 = ag1f_o.ap().rearrange("(a r b p) t -> a r b p t", a=4, r=4, b=4)
                    e.dma_start(out=scf3[:, :, j * T:(j + 1) * T],
                                in_=v5[j, bass.ts(regs["rk"], 1), :, :, :]
                                ).then_inc(ch["scp"].sem, 16)
                SY(cpf)
                ch["scp"].inc(16)
                def cpb(e, j=j):
                    v5 = ag1b_o.ap().rearrange("(a r b p) t -> a r b p t", a=4, r=4, b=4)
                    e.dma_start(out=scb3[:, :, j * T:(j + 1) * T],
                                in_=v5[j, bass.ts(regs["rk"], 1), :, :, :]
                                ).then_inc(ch["scp"].sem, 16)
                SY(cpb)
                ch["scp"].inc(16)

        def scan_block_gen():
            i = st["i"]
            if i >= NBLK:
                return
            st["i"] += 1
            if i == 0:
                if debug:
                    W("sync", ch["dbgs"], ch["dbgs"].n)
                stage_scan()
            stage_n = 8 * 16
            tc, k = i // 4, i % 4
            j = (tc * TCH) // T
            col = (tc * TCH) % T
            gcol = tc * TCH
            sl = i % 2
            if k == 0:
                if tc > 0:
                    W("sync", ch["v_sc"], 4 * tc)
                src_b = agc_o.ap()[32 * j:32 * j + 16, col:col + TCH]
                src_c = agc_o.ap()[32 * j + 16:32 * j + 32, col:col + TCH]
                ld("sync", bbc3, src_b.unsqueeze(0).broadcast_to((128, 16, TCH)), ch["scb"])
                ld("sync", cbc3, src_c.unsqueeze(0).broadcast_to((128, 16, TCH)), ch["scb"])
            scb_n = ch["scb"].n
            if i >= 2:
                W("sync", ch["v_sc"], i - 1)
                W("sync", ch["s_sc"], i - 1)
            if i == 0:
                W("sync", ch["scp"], stage_n)
                W("sync", ch["te"], te_guard_mask[0])
            ld("sync", dltf[:, sl * TCH:(sl + 1) * TCH],
               sc_f.ap()[128 * k:128 * (k + 1), gcol:gcol + TCH], ch_scf[sl])
            ld("sync", xsc[sl],
               sc_b.ap()[128 * k:128 * (k + 1), gcol:gcol + TCH], ch_scf[sl])
            scf_n = ch_scf[sl].n
            W("s", ch_scf[sl], scf_n)
            if i >= 1:
                W("s", ch["v_sc"], i)
            for n in range(16):
                SC(lambda e, n=n, k=k, sl=sl: e.activation(
                    out=dA_s[:, n * TCH:(n + 1) * TCH],
                    in_=dltf[:, sl * TCH:(sl + 1) * TCH], func=AF.Exp,
                    scale=a_sb[:, k * 16 + n:k * 16 + n + 1]))
            AT("s", ch["s_sc"])
            s_n = ch["s_sc"].n
            W("v", ch_scf[sl], scf_n)
            W("v", ch["scb"], scb_n)
            V(lambda e, sl=sl: e.tensor_tensor(out=zsc, in0=dltf[:, sl * TCH:(sl + 1) * TCH], in1=xsc[sl], op=ALU.mult))
            z3 = zsc.unsqueeze(1).broadcast_to((128, 16, TCH))
            V(lambda e, z3=z3: e.tensor_tensor(out=u_s.rearrange("p (n t) -> p n t", n=16), in0=z3, in1=bbc3, op=ALU.mult))
            yield
            W("v", ch["s_sc"], s_n)
            for n in range(16):
                V(lambda e, n=n, k=k: e.tensor_tensor_scan(
                    out=trj[:, n * TCH:(n + 1) * TCH],
                    data0=dA_s[:, n * TCH:(n + 1) * TCH],
                    data1=u_s[:, n * TCH:(n + 1) * TCH],
                    initial=hstate[:, k * 16 + n:k * 16 + n + 1],
                    op0=ALU.mult, op1=ALU.add))
                if n in (3, 7, 11):
                    yield
            V(lambda e, k=k: e.tensor_copy(
                out=hstate[:, k * 16:(k + 1) * 16], in_=trj3[:, :, TCH - 1:TCH]))
            AT("v", ch["v_sc"])
            yield
            V(lambda e: e.tensor_tensor(out=m3w, in0=trj[:], in1=cbc[:], op=ALU.mult))
            if i >= 2:
                W("v", ch_yst[i % 2], 16 * (i // 2))
            yield
            def _red(e, sl=sl):
                with nc.allow_low_precision(reason="fp32-internal reduce, bf16 store"):
                    return e.reduce_sum(out=ysc[sl], in_=m3r, axis=AX.X)
            V(_red)
            AT("v", ch["v_ys"])
            vys = ch["v_ys"].n
            W("sync", ch["v_ys"], vys)
            ld("sync", ag2_i.ap()[k * 128:(k + 1) * 128, tc * TCH:(tc + 1) * TCH],
               ysc[sl], ch_yst[sl])

        chunk_info = {}
        wd_info = {}
        for fg in range(nfg):
            pairs = [(ei, fc) for ei in range(4) for fc in range(2)]

            def emit_chunk(cidx):
                p = cidx // 2
                is_u = cidx % 2
                pi = p % 8
                ei, fc = pairs[pi]
                fgl = p // 8
                fidx = fgl * 2 + fc
                slot = wring[cidx % 3]
                late = fgl >= 4
                if cidx >= 3:
                    W("gps", ch["te_gu"],
                      (cidx - 2) if (cidx - 3) % 2 == 0 else (cidx - 1))
                if late:
                    if fgl == 4 and fidx == 8 and not is_u:
                        W("gps", ch["agw"], 3)
                    j = fidx - 8
                    row = 28 * (j // 7) + ei * 7 + (j % 7)
                    src = (wub_o if is_u else wgb_o).ap()[row].rearrange(
                        "(p t) -> p t", p=128)
                    c = ch_wgs[cidx % 3]
                    ld("gps", slot, src, c)
                else:
                    src = (wu_h if is_u else wg_h)[ei, fidx]
                    c = ch_wg[cidx % 3]
                    ld("gps", slot, src, c, max_dma_last_dim=4096)
                chunk_info[cidx] = (c, c.n)
            c0 = fg * 16
            ncfirst = 3 if fg == 0 else 4
            for cidx in range(c0, c0 + ncfirst):
                emit_chunk(cidx)
            if fg == 0:
                W("gps", ch["mk"], mk_d_ld)
                W("gps", ch_dst[0], 16 * 8)
                W("gps", ch_dst[1], 16 * 8)
                def _mf(e):
                    e.dma_start(out=mfm1.rearrange("o (a t) -> o a t", a=4),
                                in_=mk_d.ap()[:, :].unsqueeze(0),
                                max_dma_last_dim=4096).then_inc(mfm1_ch.sem, 16)
                GP(_mf)
                mfm1_ch.inc(16)
            late_wd = fg >= 4
            if fg > 0:
                W("gps", ch["te_y"], 16 * fg)
            for s_ in range(8):
                ei, fc = pairs[s_]
                fs = fg * FG + fc * 128
                if late_wd:
                    j = (fs - 1024) // 128
                    row = 28 * (j // 7) + ei * 7 + (j % 7)
                    ld("gps", wd_sb[:, s_ * 2048:(s_ + 1) * 2048],
                       wdb_o.ap()[row].rearrange("(p t) -> p t", p=128), ch["wds"])
                else:
                    ld("gps", wd_sb[:, s_ * 2048:(s_ + 1) * 2048],
                       wd_h[ei, fs:fs + 128, :], ch["wd"], max_dma_last_dim=4096)
            wd_info[fg] = (ch["wds"] if late_wd else ch["wd"],
                           (ch["wds"] if late_wd else ch["wd"]).n)
            for cidx in range(c0 + ncfirst, c0 + 16):
                emit_chunk(cidx)
            if fg == 0:
                W("gps", ch["ag1in"], ag1in_done)
                W("gps", ch_dst[0], 16 * 8)
                W("gps", ch_dst[1], 16 * 8)
                GP(lambda e: e.collective_compute(
                    "AllGather", ALU.bypass, replica_groups=GROUPS,
                    ins=[ag1f_i.ap()[:, :]], outs=[ag1f_o.ap()[:, :]]
                ).then_inc(ch["ag"].sem, 1))
                ch["ag"].inc()
                GP(lambda e: e.collective_compute(
                    "AllGather", ALU.bypass, replica_groups=GROUPS,
                    ins=[ag1b_i.ap()[:, :]], outs=[ag1b_o.ap()[:, :]]
                ).then_inc(ch["ag"].sem, 1))
                ch["ag"].inc()
                GP(lambda e: e.collective_compute(
                    "AllGather", ALU.bypass, replica_groups=GROUPS,
                    ins=[agc_i.ap()[:, :]], outs=[agc_o.ap()[:, :]]
                ).then_inc(ch["ag"].sem, 1))
                ch["ag"].inc()
                W("gps", ch["wcst"], wcst_done)
                for _i, _o in ((wgb_i, wgb_o), (wub_i, wub_o), (wdb_i, wdb_o)):
                    GP(lambda e, _i=_i, _o=_o: e.collective_compute(
                        "AllGather", ALU.bypass, replica_groups=[[0, 1, 2, 3, 4, 5, 6, 7]],
                        ins=[_i.ap()[:, :]], outs=[_o.ap()[:, :]]
                    ).then_inc(ch["agw"].sem, 1))
                    ch["agw"].inc()

            for pi in range(8):
                p = fg * 8 + pi
                ei, fc = pairs[pi]
                gset = pg if p % 2 == 0 else sp
                W("te", *chunk_info[2 * p])
                if p >= 2:
                    W("te", ch["v_h"], 2 * (p - 1))
                if pi == 0 and fg > 0:
                    W("te", ch["v_y"], 32 * fg)
                for c in range(NCk):
                    for th in range(2):
                        TE(lambda e, c=c, th=th, gset=gset, p=p: e.matmul(
                            out=gset[th].ap()[:],
                            lhsT=wring[(2 * p) % 3][:, c * 128:(c + 1) * 128],
                            rhs=xnT[:, c * T + th * 512:c * T + (th + 1) * 512],
                            start=(c == 0), stop=(c == NCk - 1)))
                AT("te", ch["te_gu"])
                W("te", *chunk_info[2 * p + 1])
                if p >= 1:
                    W("te", ch["v_u"], p)
                for c in range(NCk):
                    for th in range(2):
                        TE(lambda e, c=c, th=th, p=p: e.matmul(
                            out=pu[th].ap()[:],
                            lhsT=wring[(2 * p + 1) % 3][:, c * 128:(c + 1) * 128],
                            rhs=xnT[:, c * T + th * 512:c * T + (th + 1) * 512],
                            start=(c == 0), stop=(c == NCk - 1)))
                AT("te", ch["te_gu"])
                W("s", ch["te_gu"], 2 * p + 1)
                if p >= 1:
                    W("s", ch["v_h"], 2 * p - 1)
                SC(lambda e, gset=gset: e.activation(
                    out=silu_t[:, 0:512], in_=gset[0].ap()[:], func=AF.Tanh, scale=0.5))
                AT("s", ch["s_t"])
                if p >= 1:
                    W("s", ch["v_h"], 2 * p)
                SC(lambda e, gset=gset: e.activation(
                    out=silu_t[:, 512:1024], in_=gset[1].ap()[:], func=AF.Tanh, scale=0.5))
                AT("s", ch["s_t"])
                W("v", ch["te_gu"], 2 * p + 2)
                for th in range(2):
                    uA = um_t[:, th * 512:(th + 1) * 512]
                    V(lambda e, uA=uA, ei=ei, th=th: e.scalar_tensor_tensor(
                        out=uA, in0=pu[th].ap()[:], scalar=1.0,
                        in1=msk[:, ei * T + th * 512:ei * T + (th + 1) * 512],
                        op0=ALU.mult, op1=ALU.mult))
                AT("v", ch["v_u"])
                for th in range(2):
                    W("v", ch["s_t"], 2 * p + 1 + th)
                    sA = silu_t[:, th * 512:(th + 1) * 512]
                    uA = um_t[:, th * 512:(th + 1) * 512]
                    hA = h_sb[:, pi * 1024 + th * 512:pi * 1024 + (th + 1) * 512]
                    V(lambda e, sA=sA, gset=gset, th=th: e.scalar_tensor_tensor(
                        out=sA, in0=sA, scalar=1.0, in1=gset[th].ap()[:],
                        op0=ALU.add, op1=ALU.mult))
                    V(lambda e, hA=hA, sA=sA, uA=uA: e.tensor_tensor(
                        out=hA, in0=sA, in1=uA, op=ALU.mult))
                    AT("v", ch["v_h"])
                if fg >= 2:
                    scan_pump(3)
            W("te", ch["v_h"], ch["v_h"].n)
            W("te", *wd_info[fg])
            for dc in range(16):
                ybk = py if dc % 2 == 0 else pg
                yi = fg * 16 + dc
                if yi >= 2:
                    W("te", ch["v_y"], 2 * (yi - 1))
                if dc == 1:
                    W("te", ch["v_h"], ch["v_h"].n)
                for s_ in range(8):
                    for th in range(2):
                        TE(lambda e, s_=s_, dc=dc, th=th, ybk=ybk: e.matmul(
                            out=ybk[th].ap()[:],
                            lhsT=wd_sb[:, s_ * 2048 + dc * 128:s_ * 2048 + (dc + 1) * 128],
                            rhs=h_sb[:, s_ * 1024 + th * 512:s_ * 1024 + (th + 1) * 512],
                            start=(s_ == 0), stop=(s_ == 7)))
                AT("te", ch["te_y"])
                W("v", ch["te_y"], ch["te_y"].n)
                for th in range(2):
                    aslice = acc[:, dc * T + th * 512:dc * T + (th + 1) * 512]
                    V(lambda e, ybk=ybk, th=th, aslice=aslice: e.scalar_tensor_tensor(
                        out=aslice, in0=ybk[th].ap()[:], scalar=1.0, in1=aslice,
                        op0=ALU.mult, op1=ALU.add))
                    AT("v", ch["v_y"])
                if fg >= 2:
                    scan_pump(2)
        while do_scan and st["i"] < NBLK:
            scan_pump(1)
        yst_done = (ch_yst[0].n, ch_yst[1].n)
        te_y_done = ch["te_y"].n
        v_y_done = ch["v_y"].n

        if not (do_scan and do_out):
            W("gps", ch_yst[0], yst_done[0])
            W("gps", ch_yst[1], yst_done[1])
            W("gps", ch["v_h"], 16 * nfg)
            if debug:
                if do_scan:
                    W("sync", ch_yst[0], yst_done[0])
                    W("sync", ch_yst[1], yst_done[1])
                    ld("sync", dbg["y"][:, :], ag2_i.ap()[:, :], ch["dbgs"])
                W("gps", ch["dbgs"], ch["dbgs"].n)
            emit_blocks()
            return nc, dbg
        W("gps", ch_yst[0], yst_done[0])
        W("gps", ch_yst[1], yst_done[1])
        GP(lambda e: e.collective_compute(
            "AllGather", ALU.bypass, replica_groups=GROUPS,
            ins=[ag2_i.ap()[:, :]], outs=[ag2_o.ap()[:, :]]
        ).then_inc(ch["ag2"].sem, 1))
        ch["ag2"].inc()

        if debug:
            W("sync", ch_yst[0], yst_done[0])
            W("sync", ch_yst[1], yst_done[1])
            ld("sync", dbg["y"][:, :], ag2_i.ap()[:, :], ch["dbgs"])

        W("sync", ch["v_h"], 16 * nfg)
        ld("sync", twbc, tw_d.ap()[:, :].broadcast_to((128, T)), ch["twb"])
        twbc_ld = ch["twb"].n
        W("v", ch["twb"], twbc_ld)
        acc3 = acc[:].rearrange("p (c t) -> p c t", c=NCk)
        tw3 = twbc.unsqueeze(1).broadcast_to((128, NCk, T))
        V(lambda e: e.tensor_tensor(out=acc3, in0=acc3, in1=tw3, op=ALU.mult))
        AT("v", ch["v"])
        v_tw = ch["v"].n

        if debug:
            W("sync", ch["v"], v_tw)
            ld("sync", dbg["acc"][:, :], acc[:], ch["dbgs"])
            W("v", ch["dbgs"], ch["dbgs"].n)

        for half in range(2):
            W("sync", ch["te_y"], te_y_done)
            if half == 1:
                W("sync", ch["te"], ch["te"].n)
            W("sync", ch["wcst"], wcst_done)
            for c in range(16):
                ld("sync", opw_sb[:, c * 1024:(c + 1) * 1024],
                   opwb.ap()[c * 128:(c + 1) * 128, half * 1024:(half + 1) * 1024], ch["ow"])
            if half == 0:
                W("sync", ch["ag2"], 1)
                def cpy(e):
                    v3 = ag2_o.ap().rearrange("d (q t) -> d q t", q=4)
                    e.dma_start(out=y_scr.ap()[:, :],
                                in_=v3[:, bass.ts(regs["rk"], 1), :]
                                ).then_inc(ch["scp"].sem, 16)
                SY(cpy)
                ch["scp"].inc(16)
            for th in range(2):
                W("sync", ch["scp"], ch["scp"].n)
                W("sync", ch["te_y"], te_y_done)
                if not (half == 0 and th == 0):
                    W("sync", ch["te"], ch["te"].n)
                for c in range(16):
                    ld("sync", ysb[:, c * 512:(c + 1) * 512],
                       y_scr.ap()[c * 128:(c + 1) * 128, th * 512:(th + 1) * 512], ch["ysbL"])
                W("te", ch["ysbL"], ch["ysbL"].n)
                W("te", ch["ow"], ch["ow"].n)
                if half == 0 and th == 0:
                    W("te", ch["v"], v_tw)
                for m in range(8):
                    mg = half * 8 + m
                    oi = (half * 2 + th) * 8 + m
                    pyb = py[oi % 2]
                    if oi >= 2:
                        W("te", ch["v"], v_tw + (oi - 1))
                    for c in range(16):
                        TE(lambda e, c=c, m=m, half=half, pyb=pyb: e.matmul(
                            out=pyb.ap()[:], lhsT=opw_sb[:, c * 1024 + m * 128:c * 1024 + (m + 1) * 128],
                            rhs=ysb[:, c * 512:(c + 1) * 512],
                            start=(c == 0), stop=(c == 15)))
                    AT("te", ch["te"])
                    tei = ch["te"].n
                    W("s", ch["te"], tei)
                    if oi >= 2:
                        W("s", ch["v"], v_tw + (oi - 1))
                    SC(lambda e, pyb=pyb, mg=mg, oi=oi: e.activation(
                        out=relu2[:, (oi % 2) * 512:(oi % 2 + 1) * 512],
                        in_=pyb.ap()[:], func=AF.Relu,
                        bias=ob_pp[:, mg:mg + 1]))
                    AT("s", ch["s"])
                    sci = ch["s"].n
                    W("v", ch["s"], sci)
                    aslice = acc[:, mg * T + th * 512:mg * T + (th + 1) * 512]
                    xslice = xnT[:, mg * T + th * 512:mg * T + (th + 1) * 512]
                    V(lambda e, aslice=aslice, xslice=xslice, mg=mg: e.scalar_tensor_tensor(
                        out=aslice, in0=xslice, scalar=dv_pp[:, mg:mg + 1], in1=aslice,
                        op0=ALU.mult, op1=ALU.add))
                    V(lambda e, aslice=aslice, oi=oi: e.tensor_tensor(
                        out=aslice, in0=aslice,
                        in1=relu2[:, (oi % 2) * 512:(oi % 2 + 1) * 512],
                        op=ALU.add))
                    AT("v", ch["v"])
        v_opj = ch["v"].n

        TE(lambda e: e.drain())
        for tt in range(8):
            xb = xrel[:, (tt % 2) * 2048:(tt % 2 + 1) * 2048]
            ob = outst[:, (tt % 2) * 2048:(tt % 2 + 1) * 2048]
            if tt == 0:
                W("sync", ch["v_sc"], NBLK)
                W("sync", ch["s_sc"], NBLK)
            if tt >= 2:
                W("sync", ch["v_f"], 4 * (tt - 1))
            ld("sync", xb, x_in[tt * 128:(tt + 1) * 128, :], ch_xr[tt % 2])
            xrc = ch_xr[tt % 2].n
            W("te", ch["v"], v_opj)
            if tt >= 1:
                W("te", ch["v_f"], 4 * tt)
            for dc in range(16):
                bank = (pg + pu)[dc // 4]
                TE(lambda e, dc=dc, tt=tt, bank=bank: e.transpose(
                    out=bank.ap()[:, (dc % 4) * 128:(dc % 4 + 1) * 128],
                    in_=acc[:, dc * T + tt * 128:dc * T + (tt + 1) * 128],
                    identity=id_sb))
            AT("te", ch["te_f"])
            W("v", ch["te_f"], ch["te_f"].n)
            W("v", ch_xr[tt % 2], xrc)
            if tt >= 2:
                W("v", ch_out[tt % 2], 16 * (tt // 2))
            for g in range(4):
                bank = (pg + pu)[g]
                V(lambda e, g=g, ob=ob, xb=xb, bank=bank: e.tensor_tensor(
                    out=ob[:, g * 512:(g + 1) * 512], in0=bank.ap()[:],
                    in1=xb[:, g * 512:(g + 1) * 512], op=ALU.add))
                AT("v", ch["v_f"])
            W("sync", ch["v_f"], ch["v_f"].n)
            ld("sync", out_h[tt * 128:(tt + 1) * 128, :], ob, ch_out[tt % 2])

        W("gps", ch_out[0], ch_out[0].n)
        W("gps", ch_out[1], ch_out[1].n)
        if debug:
            W("gps", ch["dbgs"], ch["dbgs"].n)

        emit_blocks()
    return nc, dbg


def make_in_maps(inputs):
    x = np.asarray(inputs["L0_attn_hidden_states"])
    xpw = np.asarray(inputs["L0_attn_x_proj_weight"])
    dtw = np.asarray(inputs["L0_attn_dt_proj_weight"])
    dtbv = np.asarray(inputs["L0_attn_dt_proj_bias"])
    alog = np.asarray(inputs["L0_attn_A_log"])
    dv = np.asarray(inputs["L0_attn_D"])
    opw = np.asarray(inputs["L0_attn_out_proj_weight"])
    opbv = np.asarray(inputs["L0_attn_out_proj_bias"])
    wr = np.asarray(inputs["L0_ffn_Wr"])
    brv = np.asarray(inputs["L0_ffn_br"])
    wg = np.asarray(inputs["L0_ffn_Wg"])
    wu = np.asarray(inputs["L0_ffn_Wu"])
    wd = np.asarray(inputs["L0_ffn_Wd"])

    def pp16(v):
        return np.ascontiguousarray(v.reshape(16, 128).T)

    xpwT = np.ascontiguousarray(xpw.T)
    dtwT = np.ascontiguousarray(dtw.T)
    opwT = np.ascontiguousarray(opw.T)
    wg_t = np.ascontiguousarray(
        wg.reshape(NE, 16, 128, 64, 128).transpose(0, 3, 2, 1, 4).reshape(NE, 64, 128, D))
    wu_t = np.ascontiguousarray(
        wu.reshape(NE, 16, 128, 64, 128).transpose(0, 3, 2, 1, 4).reshape(NE, 64, 128, D))
    id32 = np.eye(128, dtype=np.float32)
    oneh = np.full((1, 128), 0.5, dtype=ml_dtypes.bfloat16)
    in_maps = []
    for c in range(8):
        b, q = c // 4, c % 4
        r = q
        asl = alog[512 * r:512 * (r + 1)].reshape(4, 128, 16)
        asl = np.ascontiguousarray(asl.transpose(1, 0, 2).reshape(128, 64))
        in_maps.append(dict(
            x=np.ascontiguousarray(x[b, q * T:(q + 1) * T, :]),
            xpwT=xpwT, dtwT=dtwT,
            dtb=pp16(dtbv), aslc=asl,
            dvec=pp16(dv), opwT=opwT, opb=pp16(opbv),
            wr=np.ascontiguousarray(wr.reshape(16, 128, 4).transpose(1, 0, 2).reshape(128, 64)),
            br=brv.reshape(NE, 1),
            wg=wg_t, wu=wu_t, wd=wd, id32=id32, oneh=oneh,
        ))
    return in_maps


def kernel(**inputs):
    in_maps = make_in_maps(inputs)
    nc, _ = build_program(debug=False)
    res = run_bass_kernel_spmd(nc, in_maps, core_ids=list(range(8)))
    out = np.empty((2, L, D), dtype=np.float32)
    for c in range(8):
        b, q = c // 4, c % 4
        out[b, q * T:(q + 1) * T, :] = res.results[c]["out"]
    return out


# version 07
# speedup vs baseline: 1.1135x; 1.1135x over previous
import sys
try:
    import concourse.bass
except ImportError:
    sys.path.insert(0, "/opt/trn_rl_repo")
import numpy as np
import ml_dtypes
import concourse.bass as bass
import concourse.mybir as mybir
from concourse.bass_utils import run_bass_kernel_spmd

F32 = mybir.dt.float32
BF16 = mybir.dt.bfloat16
FP8 = mybir.dt.float8e4
AF = mybir.ActivationFunctionType
ALU = mybir.AluOpType
AX = mybir.AxisListType
GROUPS = [[0, 1, 2, 3], [4, 5, 6, 7]]

D, NST, RDT, NE, FF = 2048, 16, 128, 4, 8192
T, L = 1024, 4096
NCk = 16
TCH, NTC = 128, 32
FG, NFG = 256, 32
NBLK = 4 * NTC
AGB = 2176


class Chan:
    def __init__(self, es, nc, name):
        self.sem = es.enter_context(nc.semaphore(name))
        self.n = 0

    def inc(self, k=1):
        self.n += k
        return self.n


class VChan:
    def __init__(self, oc):
        self.oc = oc
        self.n = 0
        self.marks = [0]

    def mark(self):
        self.n += 1
        self.marks.append(self.oc.n)
        return self.n


def build_program(debug=False, do_scan=True, do_out=True, nfg=NFG):
    nc = bass.Bass("TRN2", target_bir_lowering=False, debug=False)

    x_in = nc.dram_tensor("x", [T, D], F32, kind="ExternalInput").ap()
    xpwT = nc.dram_tensor("xpwT", [D, 160], F32, kind="ExternalInput").ap()
    dtwT = nc.dram_tensor("dtwT", [RDT, D], F32, kind="ExternalInput").ap()
    dtb = nc.dram_tensor("dtb", [128, 16], F32, kind="ExternalInput").ap()
    aslc = nc.dram_tensor("aslc", [128, 64], F32, kind="ExternalInput").ap()
    dvec = nc.dram_tensor("dvec", [128, 16], F32, kind="ExternalInput").ap()
    opwTh = nc.dram_tensor("opwT", [D, D], F32, kind="ExternalInput").ap()
    opb = nc.dram_tensor("opb", [128, 16], F32, kind="ExternalInput").ap()
    wr_h = nc.dram_tensor("wr", [128, 64], F32, kind="ExternalInput").ap()
    br_h = nc.dram_tensor("br", [NE, 1], F32, kind="ExternalInput").ap()
    wg_h = nc.dram_tensor("wg", [NE, FF // 128, 128, D], F32, kind="ExternalInput").ap()
    wu_h = nc.dram_tensor("wu", [NE, FF // 128, 128, D], F32, kind="ExternalInput").ap()
    wd_h = nc.dram_tensor("wd", [NE, FF, D], F32, kind="ExternalInput").ap()
    id32 = nc.dram_tensor("id32", [128, 128], F32, kind="ExternalInput").ap()
    oneh = nc.dram_tensor("oneh", [1, 128], BF16, kind="ExternalInput").ap()

    out_h = nc.dram_tensor("out", [T, D], F32, kind="ExternalOutput").ap()

    ag1f_i = nc.dram_tensor("ag1f_i", [D, T], F32, kind="Internal", addr_space="Local")
    ag1f_o = nc.dram_tensor("ag1f_o", [4 * D, T], F32, kind="Internal", addr_space="Local")
    ag1b_i = nc.dram_tensor("ag1b_i", [D, T], BF16, kind="Internal", addr_space="Local")
    ag1b_o = nc.dram_tensor("ag1b_o", [4 * D, T], BF16, kind="Internal", addr_space="Local")
    agc_i = nc.dram_tensor("agc_i", [32, T], BF16, kind="Internal", addr_space="Local")
    agc_o = nc.dram_tensor("agc_o", [128, T], BF16, kind="Internal", addr_space="Local")
    ag2_i = nc.dram_tensor("ag2_i", [512, L], BF16, kind="Internal", addr_space="Local")
    tw_d = nc.dram_tensor("tw_d", [1, T], F32, kind="Internal", addr_space="Local")
    mk_d = nc.dram_tensor("mk_d", [4, T], F32, kind="Internal", addr_space="Local")
    sc_f = nc.dram_tensor("sc_f", [512, L], F32, kind="Internal", addr_space="Local")
    sc_b = nc.dram_tensor("sc_b", [512, L], BF16, kind="Internal", addr_space="Local")
    y_scr = nc.dram_tensor("y_scr", [D, T], BF16, kind="Internal", addr_space="Local")
    ag2_o = nc.dram_tensor("ag2_o", [D, L], BF16, kind="Internal", addr_space="Local")

    dbg = {}
    if debug:
        def dbgt(name, shape, dt):
            dbg[name] = nc.dram_tensor("dbg_" + name, shape, dt, kind="ExternalOutput").ap()
        dbgt("xnT", [128, NCk * T], BF16)
        dbgt("delta", [D, T], F32)
        dbgt("bc", [32, T], BF16)
        dbgt("y", [512, L], BF16)
        dbgt("logits", [4, T], F32)
        dbgt("mask", [4, T], F32)
        dbgt("topw", [1, T], F32)
        dbgt("acc", [128, NCk * T], F32)

    from contextlib import ExitStack
    with ExitStack() as ES:
        e_ = ES.enter_context

        xnT = e_(nc.sbuf_tensor("xnT", [128, NCk * T], BF16))
        acc = e_(nc.sbuf_tensor("acc", [128, NCk * T], F32))
        arena = e_(nc.sbuf_tensor("arena", [128, 30720], BF16))
        wd_sb = arena[:, 0:16384]
        h_sb = arena[:, 16384:24576]
        wring = [arena[:, 24576 + i * 2048:24576 + (i + 1) * 2048] for i in range(3)]
        opw_sb = arena[:, 0:16384]
        ysb = arena[:, 16384:24576]

        scn = e_(nc.sbuf_tensor("scn", [128, 16384], BF16))
        dA_s = scn[:, 0:2048]
        u_s = scn[:, 2048:4096]
        trj = scn[:, 4096:6144]
        m_s = scn[:, 6144:8192]
        bbc = scn[:, 8192:10240]
        cbc = scn[:, 10240:12288]
        xsc = [scn[:, 12288 + i * TCH:12288 + (i + 1) * TCH] for i in range(2)]
        zsc = scn[:, 12544:12672]
        ysc = [scn[:, 12672 + i * TCH:12672 + (i + 1) * TCH] for i in range(2)]
        dltf = scn[:, 13312:14336].bitcast(F32)
        xtm = scn[:, 0:8192].bitcast(F32)
        sqscr = scn[:, 8192:12288].bitcast(F32)
        dstage = scn[:, 12288:16384].bitcast(F32)
        xrel = scn[:, 0:8192].bitcast(F32)
        outst = scn[:, 8192:16384].bitcast(F32)
        mfm1 = scn[0:1, 12288:16384]
        cmb_f = sqscr[0:5, 1024:2048]

        spf = e_(nc.sbuf_tensor("spf", [128, 1416], F32))
        stats = spf[:, 0:8]
        ppv = spf[:, 8:56]
        dtb_pp, dv_pp, ob_pp = ppv[:, 0:16], ppv[:, 16:32], ppv[:, 32:48]
        a_sb = spf[:, 56:120]
        lgtm = spf[:, 120:152]
        mtm = spf[:, 152:192]
        hstate = spf[:, 192:256]
        id_sb = spf[:, 256:384]
        lgfm = spf[0:4, 384:1408]

        spb = e_(nc.sbuf_tensor("spb", [128, 6848], BF16))
        msk = spb[:, 0:2048].bitcast(FP8)
        twbc = spb[:, 0:2048].bitcast(F32)
        silu_t = spb[:, 2048:3072]
        um_t = spb[:, 3072:4096]
        relu2 = spb[:, 2048:3072]
        dtlT = spb[:, 4608:5632]
        bc_sb = spb[0:32, 5632:6656]
        wr_sb = spb[:, 6656:6720]
        oh_sb = spb[0:1, 6720:6848]

        br_sb = spf[0:4, 7:8]
        eps_ap = spf[:, 1408:1409]

        pg = [e_(nc.psum_tensor(f"pg{i}", [128, 512], F32)) for i in range(2)]
        pu = [e_(nc.psum_tensor(f"pu{i}", [128, 512], F32)) for i in range(2)]
        py = [e_(nc.psum_tensor(f"py{i}", [128, 512], F32)) for i in range(2)]
        sp = [e_(nc.psum_tensor(f"sp{i}", [128, 512], F32)) for i in range(2)]

        ch = {}
        for name in ("x", "wsm", "tp", "te", "ag1in",
                     "ag", "wd", "te_gu", "te_y",
                     "scb", "ag2",
                     "ysbL", "ow", "te_f", "scp", "mk", "twb", "mf1", "dbgs",
                     "xt0", "xt1", "dst0", "dst1", "scf0", "scf1",
                     "wg0", "wg1", "wg2", "yst0", "yst1", "out0", "out1",
                     "xr0", "xr1"):
            ch[name] = Chan(ES, nc, "c_" + name)
        ch_xt = [ch["xt0"], ch["xt1"]]
        ch_dst = [ch["dst0"], ch["dst1"]]
        ch_scf = [ch["scf0"], ch["scf1"]]
        ch_wg = [ch["wg0"], ch["wg1"], ch["wg2"]]
        ch_yst = [ch["yst0"], ch["yst1"]]
        ch_out = [ch["out0"], ch["out1"]]
        ch_xr = [ch["xr0"], ch["xr1"]]

        Q = {k: [] for k in ("sync", "gps", "te", "v", "s")}
        regs = {}
        last = {k: [None] for k in Q}
        order = {"v": Chan(ES, nc, "ord_v"), "s": Chan(ES, nc, "ord_s")}
        for name in ("ln", "xnT", "v", "v_h", "v_y", "v_sc", "v_ys", "v_f", "v_u"):
            ch[name] = VChan(order["v"])
        for name in ("s", "s_t", "s_sc"):
            ch[name] = VChan(order["s"])

        def EM(q, fn):
            if q in order:
                oc = order[q]
                n = oc.n
                if n > 0:
                    Q[q].append(lambda e, sem=oc.sem, val=n: e.wait_ge(sem, val))

                def run(e, fn=fn, q=q, oc=oc):
                    ins = fn(e)
                    last[q][0] = ins
                    ins.then_inc(oc.sem, 1)
                    return ins
                Q[q].append(run)
                oc.inc()
            else:
                def run(e, fn=fn, q=q):
                    last[q][0] = fn(e)
                Q[q].append(run)

        def SY(fn): EM("sync", fn)
        def GP(fn): EM("gps", fn)
        def TE(fn): EM("te", fn)
        def V(fn): EM("v", fn)
        def SC(fn): EM("s", fn)

        def AT(q, c, k=1):
            if isinstance(c, VChan):
                assert k == 1
                c.mark()
            else:
                Q[q].append(lambda e, q=q, c=c, k=k: last[q][0].then_inc(c.sem, k))
                c.inc(k)

        def emit_blocks():
            with nc.Block() as block:
                @block.sync
                def _(e):
                    for fn in Q["sync"]:
                        fn(e)

                @block.gpsimd
                def _(e):
                    for fn in Q["gps"]:
                        fn(e)

                @block.tensor
                def _(e):
                    for fn in Q["te"]:
                        fn(e)

                @block.vector
                def _(e):
                    for fn in Q["v"]:
                        fn(e)

                @block.scalar
                def _(e):
                    for fn in Q["s"]:
                        fn(e)

        def W(q, c, v=None):
            val = c.n if v is None else v
            if val <= 0:
                return
            if isinstance(c, VChan):
                oval = c.marks[val]
                if oval <= 0:
                    return
                Q[q].append(lambda e, sem=c.oc.sem, val=oval: e.wait_ge(sem, val))
            else:
                Q[q].append(lambda e, sem=c.sem, val=val: e.wait_ge(sem, val))

        def ld(q, out, in_, c, **kw):
            def run(e, out=out, in_=in_, kw=kw, sem=c.sem, q=q):
                last[q][0] = e.dma_start(out=out, in_=in_, **kw).then_inc(sem, 16)
            Q[q].append(run)
            c.inc(16)



        def _rk(e):
            regs["rk"] = e.cc_rank(GROUPS)
        SY(_rk)

        ld("sync", dtb_pp, dtb[:, :], ch["x"])
        ld("sync", dv_pp, dvec[:, :], ch["x"])
        ld("sync", ob_pp, opb[:, :], ch["x"])
        ld("sync", a_sb, aslc[:, :], ch["x"])
        ld("sync", br_sb, br_h[:, :], ch["x"])
        ld("sync", id_sb, id32[:, :], ch["x"])
        smalls = ch["x"].n

        ld("gps", (xpw_sb := spb[:, 0:2560]).rearrange("p (c m) -> p c m", c=16), xpwT[:, :].rearrange("(c p) m -> p c m", p=128), ch["wsm"])
        ld("gps", dtw_sb := spb[:, 2560:4608], dtwT[:, :], ch["wsm"], max_dma_last_dim=4096)
        ld("gps", wr_sb, wr_h[:, :], ch["wsm"])
        ld("gps", oh_sb, oneh[:, :], ch["wsm"])
        wsm_done = ch["wsm"].n

        V(lambda e: e.memset(eps_ap, 1e-5))
        W("s", ch["x"], smalls)
        SC(lambda e: e.activation(out=sqscr[:, 0:64], in_=a_sb, func=AF.Exp))
        AT("s", ch["s"])
        a_sc = ch["s"].n
        W("v", ch["s"], a_sc)
        V(lambda e: e.tensor_scalar_mul(out=a_sb, in0=sqscr[:, 0:64], scalar1=-1.0))
        AT("v", ch["v"])

        xnT3 = xnT[:].rearrange("p (c t) -> p c t", c=NCk)
        for tt in range(8):
            b = xtm[:, (tt % 2) * 2048:(tt % 2 + 1) * 2048]
            if tt >= 2:
                W("sync", ch["tp"], tt - 1)
            ld("sync", b, x_in[tt * 128:(tt + 1) * 128, :], ch_xt[tt % 2])
            W("v", ch_xt[tt % 2], ch_xt[tt % 2].n)
            V(lambda e, b=b: e.reduce_sum(out=stats[:, 0:1], in_=b, axis=AX.X))
            V(lambda e: e.tensor_scalar_mul(out=stats[:, 1:2], in0=stats[:, 0:1], scalar1=1.0 / D))
            V(lambda e, b=b: e.tensor_scalar(out=b, in0=b, scalar1=stats[:, 1:2], scalar2=None, op0=ALU.subtract))
            AT("v", ch["v"])
            vc = ch["v"].n
            W("s", ch["v"], vc)
            SC(lambda e, b=b: e.activation(out=sqscr[:, 0:1024], in_=b[:, 0:1024], func=AF.Square, accum_out=stats[:, 2:3]))
            SC(lambda e, b=b: e.activation(out=sqscr[:, 1024:2048], in_=b[:, 1024:2048], func=AF.Square, accum_out=stats[:, 3:4]))
            AT("s", ch["s"])
            sc1 = ch["s"].n
            W("v", ch["s"], sc1)
            V(lambda e: e.tensor_add(out=stats[:, 4:5], in0=stats[:, 2:3], in1=stats[:, 3:4]))
            V(lambda e: e.tensor_scalar_mul(out=stats[:, 4:5], in0=stats[:, 4:5], scalar1=1.0 / D))
            AT("v", ch["v"])
            vc2 = ch["v"].n
            W("s", ch["v"], vc2)
            SC(lambda e: e.activation(out=stats[:, 5:6], in_=stats[:, 4:5], func=AF.Sqrt, bias=eps_ap))
            AT("s", ch["s"])
            sc2 = ch["s"].n
            W("v", ch["s"], sc2)
            V(lambda e: e.reciprocal(out=stats[:, 6:7], in_=stats[:, 5:6]))
            V(lambda e, b=b: e.tensor_scalar(out=b, in0=b, scalar1=stats[:, 6:7], scalar2=None, op0=ALU.mult))
            AT("v", ch["ln"])
            W("te", ch["ln"], ch["ln"].n)
            if tt == 0:
                W("te", ch["x"], smalls)
            if tt > 0:
                W("te", ch["xnT"], tt)
            for dc in range(16):
                bank = (pg + pu)[dc // 4]
                TE(lambda e, b=b, dc=dc, bank=bank: e.transpose(
                    out=bank.ap()[:, (dc % 4) * 128:(dc % 4 + 1) * 128],
                    in_=b[:, dc * 128:(dc + 1) * 128], identity=id_sb))
            AT("te", ch["tp"])
            W("v", ch["tp"], ch["tp"].n)
            for g in range(4):
                bank = (pg + pu)[g]
                V(lambda e, g=g, bank=bank, tt=tt: e.tensor_copy(
                    out=xnT3[:, g * 4:(g + 1) * 4, tt * 128:(tt + 1) * 128],
                    in_=bank.ap()[:]))
            AT("v", ch["xnT"])
        xnT_done = ch["xnT"].n

        TE(lambda e: e.drain())

        W("sync", ch["xnT"], xnT_done)
        ld("sync", ag1b_i.ap()[0:D, :].rearrange("(c p) t -> p c t", p=128), xnT[:].rearrange("p (c t) -> p c t", c=16), ch["ag1in"])

        W("te", ch["wsm"], wsm_done)
        W("te", ch["xnT"], xnT_done)
        for th in range(2):
            for c in range(NCk):
                TE(lambda e, th=th, c=c: e.matmul(
                    out=py[th].ap()[:], lhsT=xpw_sb[:, c * 160:c * 160 + 128],
                    rhs=xnT[:, c * T + th * 512:c * T + (th + 1) * 512],
                    start=(c == 0), stop=(c == NCk - 1)))
        AT("te", ch["te"])
        te_dtl = ch["te"].n
        W("v", ch["te"], te_dtl)
        for th in range(2):
            V(lambda e, th=th: e.tensor_copy(out=dtlT[:, th * 512:(th + 1) * 512], in_=py[th].ap()[:]))
        AT("v", ch["v"])
        v_dtl = ch["v"].n

        for th in range(2):
            for c in range(NCk):
                TE(lambda e, th=th, c=c: e.matmul(
                    out=pg[th].ap()[0:32, :], lhsT=xpw_sb[:, c * 160 + 128:c * 160 + 160],
                    rhs=xnT[:, c * T + th * 512:c * T + (th + 1) * 512],
                    start=(c == 0), stop=(c == NCk - 1)))
        AT("te", ch["te"])
        te_bc = ch["te"].n
        W("v", ch["te"], te_bc)
        for th in range(2):
            V(lambda e, th=th: e.tensor_copy(out=bc_sb[:, th * 512:(th + 1) * 512], in_=pg[th].ap()[0:32, :]))
        AT("v", ch["v"])
        v_bc = ch["v"].n
        W("sync", ch["v"], v_bc)
        ld("sync", agc_i.ap()[:, :], bc_sb, ch["ag1in"])

        W("te", ch["v"], v_dtl)
        W("s", ch["x"], smalls)
        v_base = ch["v"].n
        for c2 in range(NCk):
            for th in range(2):
                u = 2 * c2 + th
                pb = pu[u % 2]
                if u >= 2:
                    W("te", ch["v"], v_base + (u - 1))
                TE(lambda e, c2=c2, th=th, pb=pb: e.matmul(
                    out=pb.ap()[:], lhsT=dtw_sb[:, c2 * 128:(c2 + 1) * 128],
                    rhs=dtlT[:, th * 512:(th + 1) * 512], start=True, stop=True))
                AT("te", ch["te"])
                tec = ch["te"].n
                W("s", ch["te"], tec)
                if u >= 2:
                    W("s", ch["v"], v_base + (u - 1))
                sp1 = sqscr[:, (u % 2) * 512:(u % 2) * 512 + 512]
                SC(lambda e, pb=pb, c2=c2, sp1=sp1: e.activation(
                    out=sp1, in_=pb.ap()[:], func=AF.Abs, bias=dtb_pp[:, c2:c2 + 1]))
                SC(lambda e, sp1=sp1: e.activation(out=sp1, in_=sp1, func=AF.Exp, scale=-1.0))
                SC(lambda e, sp1=sp1: e.activation(out=sp1, in_=sp1, func=AF.Ln, bias=1.0))
                AT("s", ch["s"])
                scc = ch["s"].n
                dsl = dstage[:, (c2 % 2) * 1024 + th * 512:(c2 % 2) * 1024 + (th + 1) * 512]
                W("v", ch["s"], scc)
                if c2 >= 2 and th == 0:
                    W("v", ch_dst[c2 % 2], 16 * (c2 // 2))
                V(lambda e, pb=pb, c2=c2, dsl=dsl: e.tensor_scalar(
                    out=dsl, in0=pb.ap()[:], scalar1=dtb_pp[:, c2:c2 + 1],
                    scalar2=0.0, op0=ALU.add, op1=ALU.max))
                V(lambda e, dsl=dsl, sp1=sp1: e.tensor_add(out=dsl, in0=dsl, in1=sp1))
                AT("v", ch["v"])
            W("sync", ch["v"], ch["v"].n)
            ld("sync", ag1f_i.ap()[c2 * 128:(c2 + 1) * 128, :],
               dstage[:, (c2 % 2) * 1024:(c2 % 2) * 1024 + 1024], ch_dst[c2 % 2])
        ag1in_done = ch["ag1in"].n

        W("te", ch["v"], v_dtl)
        for th in range(2):
            for c in range(NCk):
                TE(lambda e, th=th, c=c: e.matmul(
                    out=py[th].ap()[0:4, :], lhsT=wr_sb[:, c * 4:(c + 1) * 4],
                    rhs=xnT[:, c * T + th * 512:c * T + (th + 1) * 512],
                    start=(c == 0), stop=(c == NCk - 1)))
        AT("te", ch["te"])
        te_lg = ch["te"].n
        W("v", ch["te"], te_lg)
        for th in range(2):
            V(lambda e, th=th: e.tensor_scalar(
                out=lgfm[:, th * 512:(th + 1) * 512], in0=py[th].ap()[0:4, :],
                scalar1=br_sb, scalar2=None, op0=ALU.add))
        AT("v", ch["v"])
        v_lg = ch["v"].n

        TE(lambda e: e.drain())
        W("te", ch["v"], v_lg)
        for tt in range(8):
            TE(lambda e, tt=tt: e.transpose(
                out=pg[0].ap()[:, tt * 4:(tt + 1) * 4],
                in_=lgfm[:, tt * 128:(tt + 1) * 128], identity=id_sb[0:4, 0:4]))
        AT("te", ch["te"])
        te_lgt = ch["te"].n
        W("v", ch["te"], te_lgt)
        V(lambda e: e.tensor_copy(out=lgtm, in_=pg[0].ap()[:, 0:32]))
        for tt in range(8):
            lg4 = lgtm[:, tt * 4:(tt + 1) * 4]
            V(lambda e, lg4=lg4: e.reduce_max(out=stats[:, 0:1], in_=lg4, axis=AX.X))
            V(lambda e: e.tensor_scalar_mul(out=stats[:, 1:2], in0=stats[:, 0:1], scalar1=-1.0))
            AT("v", ch["v"])
            vmx = ch["v"].n
            W("s", ch["v"], vmx)
            SC(lambda e, lg4=lg4: e.activation(
                out=sqscr[:, 0:4], in_=lg4, func=AF.Exp,
                bias=stats[:, 1:2], accum_out=stats[:, 2:3]))
            AT("s", ch["s"])
            sse = ch["s"].n
            W("v", ch["s"], sse)
            V(lambda e, tt=tt: e.reciprocal(out=mtm[:, tt * 5 + 4:tt * 5 + 5], in_=stats[:, 2:3]))
            V(lambda e: e.memset(stats[:, 3:4], 1.0))
            for ei in range(4):
                V(lambda e, lg4=lg4, ei=ei: e.tensor_scalar(
                    out=stats[:, 4:5], in0=lg4[:, ei:ei + 1], scalar1=stats[:, 0:1],
                    scalar2=None, op0=ALU.is_equal))
                V(lambda e, tt=tt, ei=ei: e.tensor_mul(
                    out=mtm[:, tt * 5 + ei:tt * 5 + ei + 1], in0=stats[:, 4:5], in1=stats[:, 3:4]))
                V(lambda e, tt=tt, ei=ei: e.tensor_sub(
                    out=stats[:, 3:4], in0=stats[:, 3:4], in1=mtm[:, tt * 5 + ei:tt * 5 + ei + 1]))
        AT("v", ch["v"])
        v_msk = ch["v"].n
        W("te", ch["v"], v_msk)
        for tt in range(8):
            TE(lambda e, tt=tt: e.transpose(
                out=pu[tt // 4].ap()[0:5, (tt % 4) * 128:(tt % 4 + 1) * 128],
                in_=mtm[:, tt * 5:tt * 5 + 5], identity=id_sb))
        AT("te", ch["te"])
        te_mkt = ch["te"].n
        TE(lambda e: e.drain())
        W("v", ch["te"], te_mkt)
        for g in range(2):
            V(lambda e, g=g: e.tensor_copy(out=cmb_f[:, g * 512:(g + 1) * 512], in_=pu[g].ap()[0:5, :]))
        AT("v", ch["v"])
        v_mfm = ch["v"].n
        W("sync", ch["v"], v_mfm)
        ld("sync", tw_d.ap()[:, :], cmb_f[4:5, :], ch["mk"])
        ld("sync", mk_d.ap()[:, :], cmb_f[0:4, :], ch["mk"])
        mk_d_ld = ch["mk"].n
        mfm1_ch = ch["mf1"]
        W("te", ch["v"], v_mfm)
        W("te", mfm1_ch, 16)
        for ei in range(4):
            if ei > 0:
                W("te", ch["v"], ch["v"].n)
            for th in range(2):
                TE(lambda e, ei=ei, th=th: e.matmul(
                    out=py[th].ap()[:], lhsT=oh_sb,
                    rhs=mfm1[0:1, ei * 1024 + th * 512:ei * 1024 + (th + 1) * 512],
                    start=True, stop=True))
            AT("te", ch["te"])
            temb = ch["te"].n
            W("v", ch["te"], temb)
            for th in range(2):
                V(lambda e, ei=ei, th=th: e.tensor_copy(
                    out=msk[:, ei * T + th * 512:ei * T + (th + 1) * 512],
                    in_=py[th].ap()[:]))
            AT("v", ch["v"])
        v_mskb = ch["v"].n
        te_guard_mask = [ch["te"].n]

        if debug:
            W("sync", ch["v"], v_mskb)
            W("sync", ch["mk"], ch["mk"].n)
            ld("sync", dbg["logits"][:, :], lgfm, ch["dbgs"])
            ld("sync", dbg["mask"][:, :], cmb_f[0:4, :], ch["dbgs"])
            ld("sync", dbg["topw"][:, :], tw_d.ap()[:, :], ch["dbgs"])
            ld("sync", dbg["xnT"][:, :], xnT[:], ch["dbgs"])
            W("sync", ch_dst[0], 16 * 8)
            W("sync", ch_dst[1], 16 * 8)
            ld("sync", dbg["delta"][:, :], ag1f_i.ap()[:, :], ch["dbgs"])
            W("sync", ch["ag1in"], ch["ag1in"].n)
            ld("sync", dbg["bc"][:, :], agc_i.ap()[:, :], ch["dbgs"])

        W("v", ch["x"], smalls)
        V(lambda e: e.memzero(hstate))
        V(lambda e: e.memzero(acc[:]))

        pump = {"g": None}

        def scan_pump(n=1):
            if not do_scan:
                return
            for _ in range(n):
                if pump["g"] is None:
                    if st["i"] >= NBLK:
                        return
                    pump["g"] = scan_block_gen()
                try:
                    next(pump["g"])
                except StopIteration:
                    pump["g"] = None

        st = {"i": 0}
        bbc3 = bbc.rearrange("p (n t) -> p n t", n=16)
        cbc3 = cbc.rearrange("p (n t) -> p n t", n=16)
        m3w = m_s.rearrange("p (t n) -> p n t", n=16)
        m3r = m_s.rearrange("p (t n) -> p t n", n=16)
        trj3 = trj.rearrange("p (n t) -> p n t", n=16)

        def stage_scan():
            W("sync", ch["ag"], 3)
            scf3 = sc_f.ap().rearrange("(k p) t -> k p t", k=4)
            scb3 = sc_b.ap().rearrange("(k p) t -> k p t", k=4)
            for j in range(4):
                def cpf(e, j=j):
                    v5 = ag1f_o.ap().rearrange("(a r b p) t -> a r b p t", a=4, r=4, b=4)
                    e.dma_start(out=scf3[:, :, j * T:(j + 1) * T],
                                in_=v5[j, bass.ts(regs["rk"], 1), :, :, :]
                                ).then_inc(ch["scp"].sem, 16)
                SY(cpf)
                ch["scp"].inc(16)
                def cpb(e, j=j):
                    v5 = ag1b_o.ap().rearrange("(a r b p) t -> a r b p t", a=4, r=4, b=4)
                    e.dma_start(out=scb3[:, :, j * T:(j + 1) * T],
                                in_=v5[j, bass.ts(regs["rk"], 1), :, :, :]
                                ).then_inc(ch["scp"].sem, 16)
                SY(cpb)
                ch["scp"].inc(16)

        def scan_block_gen():
            i = st["i"]
            if i >= NBLK:
                return
            st["i"] += 1
            if i == 0:
                if debug:
                    W("sync", ch["dbgs"], ch["dbgs"].n)
                stage_scan()
            stage_n = 8 * 16
            tc, k = i // 4, i % 4
            j = (tc * TCH) // T
            col = (tc * TCH) % T
            gcol = tc * TCH
            sl = i % 2
            if k == 0:
                if tc > 0:
                    W("sync", ch["v_sc"], 4 * tc)
                src_b = agc_o.ap()[32 * j:32 * j + 16, col:col + TCH]
                src_c = agc_o.ap()[32 * j + 16:32 * j + 32, col:col + TCH]
                ld("sync", bbc3, src_b.unsqueeze(0).broadcast_to((128, 16, TCH)), ch["scb"])
                ld("sync", cbc3, src_c.unsqueeze(0).broadcast_to((128, 16, TCH)), ch["scb"])
            scb_n = ch["scb"].n
            if i >= 2:
                W("sync", ch["v_sc"], i - 1)
                W("sync", ch["s_sc"], i - 1)
            if i == 0:
                W("sync", ch["scp"], stage_n)
                W("sync", ch["te"], te_guard_mask[0])
            ld("sync", dltf[:, sl * TCH:(sl + 1) * TCH],
               sc_f.ap()[128 * k:128 * (k + 1), gcol:gcol + TCH], ch_scf[sl])
            ld("sync", xsc[sl],
               sc_b.ap()[128 * k:128 * (k + 1), gcol:gcol + TCH], ch_scf[sl])
            scf_n = ch_scf[sl].n
            W("s", ch_scf[sl], scf_n)
            if i >= 1:
                W("s", ch["v_sc"], i)
            for n in range(16):
                SC(lambda e, n=n, k=k, sl=sl: e.activation(
                    out=dA_s[:, n * TCH:(n + 1) * TCH],
                    in_=dltf[:, sl * TCH:(sl + 1) * TCH], func=AF.Exp,
                    scale=a_sb[:, k * 16 + n:k * 16 + n + 1]))
            AT("s", ch["s_sc"])
            s_n = ch["s_sc"].n
            W("v", ch_scf[sl], scf_n)
            W("v", ch["scb"], scb_n)
            V(lambda e, sl=sl: e.tensor_tensor(out=zsc, in0=dltf[:, sl * TCH:(sl + 1) * TCH], in1=xsc[sl], op=ALU.mult))
            for n in range(16):
                V(lambda e, n=n: e.tensor_tensor(
                    out=u_s[:, n * TCH:(n + 1) * TCH], in0=zsc,
                    in1=bbc[:, n * TCH:(n + 1) * TCH], op=ALU.mult))
            yield
            W("v", ch["s_sc"], s_n)
            for n in range(16):
                V(lambda e, n=n, k=k: e.tensor_tensor_scan(
                    out=trj[:, n * TCH:(n + 1) * TCH],
                    data0=dA_s[:, n * TCH:(n + 1) * TCH],
                    data1=u_s[:, n * TCH:(n + 1) * TCH],
                    initial=hstate[:, k * 16 + n:k * 16 + n + 1],
                    op0=ALU.mult, op1=ALU.add))
                if n in (3, 7, 11):
                    yield
            V(lambda e, k=k: e.tensor_copy(
                out=hstate[:, k * 16:(k + 1) * 16], in_=trj3[:, :, TCH - 1:TCH]))
            AT("v", ch["v_sc"])
            yield
            V(lambda e: e.tensor_tensor(out=m3w, in0=trj[:], in1=cbc[:], op=ALU.mult))
            if i >= 2:
                W("v", ch_yst[i % 2], 16 * (i // 2))
            yield
            def _red(e, sl=sl):
                with nc.allow_low_precision(reason="fp32-internal reduce, bf16 store"):
                    return e.reduce_sum(out=ysc[sl], in_=m3r, axis=AX.X)
            V(_red)
            AT("v", ch["v_ys"])
            vys = ch["v_ys"].n
            W("sync", ch["v_ys"], vys)
            ld("sync", ag2_i.ap()[k * 128:(k + 1) * 128, tc * TCH:(tc + 1) * TCH],
               ysc[sl], ch_yst[sl])

        for fg in range(nfg):
            pairs = [(ei, fc) for ei in range(4) for fc in range(2)]

            def emit_chunk(cidx):
                p = cidx // 2
                is_u = cidx % 2
                pi = p % 8
                ei, fc = pairs[pi]
                fgl = p // 8
                fidx = fgl * 2 + fc
                slot = wring[cidx % 3]
                if cidx >= 3:
                    W("gps", ch["te_gu"],
                      (cidx - 2) if (cidx - 3) % 2 == 0 else (cidx - 1))
                src = (wu_h if is_u else wg_h)[ei, fidx]
                ld("gps", slot, src, ch_wg[cidx % 3], max_dma_last_dim=4096)
            c0 = fg * 16
            ncfirst = 3 if fg == 0 else 4
            for cidx in range(c0, c0 + ncfirst):
                emit_chunk(cidx)
            if fg == 0:
                W("gps", ch["mk"], mk_d_ld)
                W("gps", ch_dst[0], 16 * 8)
                W("gps", ch_dst[1], 16 * 8)
                def _mf(e):
                    e.dma_start(out=mfm1.rearrange("o (a t) -> o a t", a=4),
                                in_=mk_d.ap()[:, :].unsqueeze(0),
                                max_dma_last_dim=4096).then_inc(mfm1_ch.sem, 16)
                GP(_mf)
                mfm1_ch.inc(16)
            if fg > 0:
                W("gps", ch["te_y"], 16 * fg)
            for s_ in range(8):
                ei, fc = pairs[s_]
                fs = fg * FG + fc * 128
                ld("gps", wd_sb[:, s_ * 2048:(s_ + 1) * 2048],
                   wd_h[ei, fs:fs + 128, :], ch["wd"], max_dma_last_dim=4096)
            for cidx in range(c0 + ncfirst, c0 + 16):
                emit_chunk(cidx)
            if fg == 0:
                W("gps", ch["ag1in"], ag1in_done)
                W("gps", ch_dst[0], 16 * 8)
                W("gps", ch_dst[1], 16 * 8)
                GP(lambda e: e.collective_compute(
                    "AllGather", ALU.bypass, replica_groups=GROUPS,
                    ins=[ag1f_i.ap()[:, :]], outs=[ag1f_o.ap()[:, :]]
                ).then_inc(ch["ag"].sem, 1))
                ch["ag"].inc()
                GP(lambda e: e.collective_compute(
                    "AllGather", ALU.bypass, replica_groups=GROUPS,
                    ins=[ag1b_i.ap()[:, :]], outs=[ag1b_o.ap()[:, :]]
                ).then_inc(ch["ag"].sem, 1))
                ch["ag"].inc()
                GP(lambda e: e.collective_compute(
                    "AllGather", ALU.bypass, replica_groups=GROUPS,
                    ins=[agc_i.ap()[:, :]], outs=[agc_o.ap()[:, :]]
                ).then_inc(ch["ag"].sem, 1))
                ch["ag"].inc()

            for pi in range(8):
                p = fg * 8 + pi
                ei, fc = pairs[pi]
                gset = pg if p % 2 == 0 else sp
                W("te", ch_wg[(2 * p) % 3], 16 * ((2 * p) // 3 + 1))
                if p >= 2:
                    W("te", ch["v_h"], 2 * (p - 1))
                if pi == 0 and fg > 0:
                    W("te", ch["v_y"], 32 * fg)
                for c in range(NCk):
                    for th in range(2):
                        TE(lambda e, c=c, th=th, gset=gset, p=p: e.matmul(
                            out=gset[th].ap()[:],
                            lhsT=wring[(2 * p) % 3][:, c * 128:(c + 1) * 128],
                            rhs=xnT[:, c * T + th * 512:c * T + (th + 1) * 512],
                            start=(c == 0), stop=(c == NCk - 1)))
                AT("te", ch["te_gu"])
                W("te", ch_wg[(2 * p + 1) % 3], 16 * ((2 * p + 1) // 3 + 1))
                if p >= 1:
                    W("te", ch["v_u"], p)
                for c in range(NCk):
                    for th in range(2):
                        TE(lambda e, c=c, th=th, p=p: e.matmul(
                            out=pu[th].ap()[:],
                            lhsT=wring[(2 * p + 1) % 3][:, c * 128:(c + 1) * 128],
                            rhs=xnT[:, c * T + th * 512:c * T + (th + 1) * 512],
                            start=(c == 0), stop=(c == NCk - 1)))
                AT("te", ch["te_gu"])
                W("s", ch["te_gu"], 2 * p + 1)
                if p >= 1:
                    W("s", ch["v_h"], 2 * p - 1)
                SC(lambda e, gset=gset: e.activation(
                    out=silu_t[:, 0:512], in_=gset[0].ap()[:], func=AF.Tanh, scale=0.5))
                AT("s", ch["s_t"])
                if p >= 1:
                    W("s", ch["v_h"], 2 * p)
                SC(lambda e, gset=gset: e.activation(
                    out=silu_t[:, 512:1024], in_=gset[1].ap()[:], func=AF.Tanh, scale=0.5))
                AT("s", ch["s_t"])
                W("v", ch["te_gu"], 2 * p + 2)
                for th in range(2):
                    uA = um_t[:, th * 512:(th + 1) * 512]
                    V(lambda e, uA=uA, ei=ei, th=th: e.scalar_tensor_tensor(
                        out=uA, in0=pu[th].ap()[:], scalar=1.0,
                        in1=msk[:, ei * T + th * 512:ei * T + (th + 1) * 512],
                        op0=ALU.mult, op1=ALU.mult))
                AT("v", ch["v_u"])
                for th in range(2):
                    W("v", ch["s_t"], 2 * p + 1 + th)
                    sA = silu_t[:, th * 512:(th + 1) * 512]
                    uA = um_t[:, th * 512:(th + 1) * 512]
                    hA = h_sb[:, pi * 1024 + th * 512:pi * 1024 + (th + 1) * 512]
                    V(lambda e, sA=sA, gset=gset, th=th: e.scalar_tensor_tensor(
                        out=sA, in0=sA, scalar=1.0, in1=gset[th].ap()[:],
                        op0=ALU.add, op1=ALU.mult))
                    V(lambda e, hA=hA, sA=sA, uA=uA: e.tensor_tensor(
                        out=hA, in0=sA, in1=uA, op=ALU.mult))
                    AT("v", ch["v_h"])
                if fg >= 2:
                    scan_pump(3)
            W("te", ch["v_h"], ch["v_h"].n)
            W("te", ch["wd"], ch["wd"].n)
            for dc in range(16):
                ybk = py if dc % 2 == 0 else pg
                yi = fg * 16 + dc
                if yi >= 2:
                    W("te", ch["v_y"], 2 * (yi - 1))
                if dc == 1:
                    W("te", ch["v_h"], ch["v_h"].n)
                for s_ in range(8):
                    for th in range(2):
                        TE(lambda e, s_=s_, dc=dc, th=th, ybk=ybk: e.matmul(
                            out=ybk[th].ap()[:],
                            lhsT=wd_sb[:, s_ * 2048 + dc * 128:s_ * 2048 + (dc + 1) * 128],
                            rhs=h_sb[:, s_ * 1024 + th * 512:s_ * 1024 + (th + 1) * 512],
                            start=(s_ == 0), stop=(s_ == 7)))
                AT("te", ch["te_y"])
                W("v", ch["te_y"], ch["te_y"].n)
                for th in range(2):
                    aslice = acc[:, dc * T + th * 512:dc * T + (th + 1) * 512]
                    V(lambda e, ybk=ybk, th=th, aslice=aslice: e.scalar_tensor_tensor(
                        out=aslice, in0=ybk[th].ap()[:], scalar=1.0, in1=aslice,
                        op0=ALU.mult, op1=ALU.add))
                    AT("v", ch["v_y"])
                if fg >= 2:
                    scan_pump(2)
        while do_scan and st["i"] < NBLK:
            scan_pump(1)
        yst_done = (ch_yst[0].n, ch_yst[1].n)
        te_y_done = ch["te_y"].n
        v_y_done = ch["v_y"].n

        if not (do_scan and do_out):
            W("gps", ch_yst[0], yst_done[0])
            W("gps", ch_yst[1], yst_done[1])
            W("gps", ch["v_h"], 16 * nfg)
            if debug:
                if do_scan:
                    W("sync", ch_yst[0], yst_done[0])
                    W("sync", ch_yst[1], yst_done[1])
                    ld("sync", dbg["y"][:, :], ag2_i.ap()[:, :], ch["dbgs"])
                W("gps", ch["dbgs"], ch["dbgs"].n)
            emit_blocks()
            return nc, dbg
        W("gps", ch_yst[0], yst_done[0])
        W("gps", ch_yst[1], yst_done[1])
        GP(lambda e: e.collective_compute(
            "AllGather", ALU.bypass, replica_groups=GROUPS,
            ins=[ag2_i.ap()[:, :]], outs=[ag2_o.ap()[:, :]]
        ).then_inc(ch["ag2"].sem, 1))
        ch["ag2"].inc()

        if debug:
            W("sync", ch_yst[0], yst_done[0])
            W("sync", ch_yst[1], yst_done[1])
            ld("sync", dbg["y"][:, :], ag2_i.ap()[:, :], ch["dbgs"])

        W("sync", ch["v_h"], 16 * nfg)
        ld("sync", twbc, tw_d.ap()[:, :].broadcast_to((128, T)), ch["twb"])
        twbc_ld = ch["twb"].n
        W("v", ch["twb"], twbc_ld)
        acc3 = acc[:].rearrange("p (c t) -> p c t", c=NCk)
        tw3 = twbc.unsqueeze(1).broadcast_to((128, NCk, T))
        V(lambda e: e.tensor_tensor(out=acc3, in0=acc3, in1=tw3, op=ALU.mult))
        AT("v", ch["v"])
        v_tw = ch["v"].n

        if debug:
            W("sync", ch["v"], v_tw)
            ld("sync", dbg["acc"][:, :], acc[:], ch["dbgs"])
            W("v", ch["dbgs"], ch["dbgs"].n)

        for half in range(2):
            W("gps", ch["te_y"], te_y_done)
            if half == 1:
                W("gps", ch["te"], ch["te"].n)
            for c in range(16):
                ld("gps", opw_sb[:, c * 1024:(c + 1) * 1024],
                   opwTh[c * 128:(c + 1) * 128, half * 1024:(half + 1) * 1024], ch["ow"])
            if half == 0:
                W("sync", ch["ag2"], 1)
                def cpy(e):
                    v3 = ag2_o.ap().rearrange("d (q t) -> d q t", q=4)
                    e.dma_start(out=y_scr.ap()[:, :],
                                in_=v3[:, bass.ts(regs["rk"], 1), :]
                                ).then_inc(ch["scp"].sem, 16)
                SY(cpy)
                ch["scp"].inc(16)
            for th in range(2):
                W("sync", ch["scp"], ch["scp"].n)
                W("sync", ch["te_y"], te_y_done)
                if not (half == 0 and th == 0):
                    W("sync", ch["te"], ch["te"].n)
                for c in range(16):
                    ld("sync", ysb[:, c * 512:(c + 1) * 512],
                       y_scr.ap()[c * 128:(c + 1) * 128, th * 512:(th + 1) * 512], ch["ysbL"])
                W("te", ch["ysbL"], ch["ysbL"].n)
                W("te", ch["ow"], ch["ow"].n)
                if half == 0 and th == 0:
                    W("te", ch["v"], v_tw)
                for m in range(8):
                    mg = half * 8 + m
                    oi = (half * 2 + th) * 8 + m
                    pyb = py[oi % 2]
                    if oi >= 2:
                        W("te", ch["v"], v_tw + (oi - 1))
                    for c in range(16):
                        TE(lambda e, c=c, m=m, half=half, pyb=pyb: e.matmul(
                            out=pyb.ap()[:], lhsT=opw_sb[:, c * 1024 + m * 128:c * 1024 + (m + 1) * 128],
                            rhs=ysb[:, c * 512:(c + 1) * 512],
                            start=(c == 0), stop=(c == 15)))
                    AT("te", ch["te"])
                    tei = ch["te"].n
                    W("s", ch["te"], tei)
                    if oi >= 2:
                        W("s", ch["v"], v_tw + (oi - 1))
                    SC(lambda e, pyb=pyb, mg=mg, oi=oi: e.activation(
                        out=relu2[:, (oi % 2) * 512:(oi % 2 + 1) * 512],
                        in_=pyb.ap()[:], func=AF.Relu,
                        bias=ob_pp[:, mg:mg + 1]))
                    AT("s", ch["s"])
                    sci = ch["s"].n
                    W("v", ch["s"], sci)
                    aslice = acc[:, mg * T + th * 512:mg * T + (th + 1) * 512]
                    xslice = xnT[:, mg * T + th * 512:mg * T + (th + 1) * 512]
                    V(lambda e, aslice=aslice, xslice=xslice, mg=mg: e.scalar_tensor_tensor(
                        out=aslice, in0=xslice, scalar=dv_pp[:, mg:mg + 1], in1=aslice,
                        op0=ALU.mult, op1=ALU.add))
                    V(lambda e, aslice=aslice, oi=oi: e.tensor_tensor(
                        out=aslice, in0=aslice,
                        in1=relu2[:, (oi % 2) * 512:(oi % 2 + 1) * 512],
                        op=ALU.add))
                    AT("v", ch["v"])
        v_opj = ch["v"].n

        TE(lambda e: e.drain())
        for tt in range(8):
            xb = xrel[:, (tt % 2) * 2048:(tt % 2 + 1) * 2048]
            ob = outst[:, (tt % 2) * 2048:(tt % 2 + 1) * 2048]
            if tt == 0:
                W("sync", ch["v_sc"], NBLK)
                W("sync", ch["s_sc"], NBLK)
            if tt >= 2:
                W("sync", ch["v_f"], 4 * (tt - 1))
            ld("sync", xb, x_in[tt * 128:(tt + 1) * 128, :], ch_xr[tt % 2])
            xrc = ch_xr[tt % 2].n
            W("te", ch["v"], v_opj)
            if tt >= 1:
                W("te", ch["v_f"], 4 * tt)
            for dc in range(16):
                bank = (pg + pu)[dc // 4]
                TE(lambda e, dc=dc, tt=tt, bank=bank: e.transpose(
                    out=bank.ap()[:, (dc % 4) * 128:(dc % 4 + 1) * 128],
                    in_=acc[:, dc * T + tt * 128:dc * T + (tt + 1) * 128],
                    identity=id_sb))
            AT("te", ch["te_f"])
            W("v", ch["te_f"], ch["te_f"].n)
            W("v", ch_xr[tt % 2], xrc)
            if tt >= 2:
                W("v", ch_out[tt % 2], 16 * (tt // 2))
            for g in range(4):
                bank = (pg + pu)[g]
                V(lambda e, g=g, ob=ob, xb=xb, bank=bank: e.tensor_tensor(
                    out=ob[:, g * 512:(g + 1) * 512], in0=bank.ap()[:],
                    in1=xb[:, g * 512:(g + 1) * 512], op=ALU.add))
                AT("v", ch["v_f"])
            W("sync", ch["v_f"], ch["v_f"].n)
            ld("sync", out_h[tt * 128:(tt + 1) * 128, :], ob, ch_out[tt % 2])

        W("gps", ch_out[0], ch_out[0].n)
        W("gps", ch_out[1], ch_out[1].n)
        if debug:
            W("gps", ch["dbgs"], ch["dbgs"].n)

        emit_blocks()
    return nc, dbg


def make_in_maps(inputs):
    x = np.asarray(inputs["L0_attn_hidden_states"])
    xpw = np.asarray(inputs["L0_attn_x_proj_weight"])
    dtw = np.asarray(inputs["L0_attn_dt_proj_weight"])
    dtbv = np.asarray(inputs["L0_attn_dt_proj_bias"])
    alog = np.asarray(inputs["L0_attn_A_log"])
    dv = np.asarray(inputs["L0_attn_D"])
    opw = np.asarray(inputs["L0_attn_out_proj_weight"])
    opbv = np.asarray(inputs["L0_attn_out_proj_bias"])
    wr = np.asarray(inputs["L0_ffn_Wr"])
    brv = np.asarray(inputs["L0_ffn_br"])
    wg = np.asarray(inputs["L0_ffn_Wg"])
    wu = np.asarray(inputs["L0_ffn_Wu"])
    wd = np.asarray(inputs["L0_ffn_Wd"])

    def pp16(v):
        return np.ascontiguousarray(v.reshape(16, 128).T)

    xpwT = np.ascontiguousarray(xpw.T)
    dtwT = np.ascontiguousarray(dtw.T)
    opwT = np.ascontiguousarray(opw.T)
    wg_t = np.ascontiguousarray(
        wg.reshape(NE, 16, 128, 64, 128).transpose(0, 3, 2, 1, 4).reshape(NE, 64, 128, D))
    wu_t = np.ascontiguousarray(
        wu.reshape(NE, 16, 128, 64, 128).transpose(0, 3, 2, 1, 4).reshape(NE, 64, 128, D))
    id32 = np.eye(128, dtype=np.float32)
    oneh = np.full((1, 128), 0.5, dtype=ml_dtypes.bfloat16)
    in_maps = []
    for c in range(8):
        b, q = c // 4, c % 4
        r = q
        asl = alog[512 * r:512 * (r + 1)].reshape(4, 128, 16)
        asl = np.ascontiguousarray(asl.transpose(1, 0, 2).reshape(128, 64))
        in_maps.append(dict(
            x=np.ascontiguousarray(x[b, q * T:(q + 1) * T, :]),
            xpwT=xpwT, dtwT=dtwT,
            dtb=pp16(dtbv), aslc=asl,
            dvec=pp16(dv), opwT=opwT, opb=pp16(opbv),
            wr=np.ascontiguousarray(wr.reshape(16, 128, 4).transpose(1, 0, 2).reshape(128, 64)),
            br=brv.reshape(NE, 1),
            wg=wg_t, wu=wu_t, wd=wd, id32=id32, oneh=oneh,
        ))
    return in_maps


def kernel(**inputs):
    in_maps = make_in_maps(inputs)
    nc, _ = build_program(debug=False)
    res = run_bass_kernel_spmd(nc, in_maps, core_ids=list(range(8)))
    out = np.empty((2, L, D), dtype=np.float32)
    for c in range(8):
        b, q = c // 4, c % 4
        out[b, q * T:(q + 1) * T, :] = res.results[c]["out"]
    return out
